# version 01
# baseline (speedup 1.0000x reference)
import math

import numpy as np
import ml_dtypes
import concourse.bass as bass
import concourse.mybir as mybir
from concourse.bass_utils import run_bass_kernel_spmd

F32 = mybir.dt.float32
BF16 = mybir.dt.bfloat16
AF = mybir.ActivationFunctionType
ALU = mybir.AluOpType
AX = mybir.AxisListType
BF16NP = ml_dtypes.bfloat16

T = 2048
D = 2048
NTT = 16
NTB = 4
NDK = 16
DIN = 4096
DST = 128
NH = 64
HP = 64
NCH = 16
CONVD = 4352
EIN = 8512
INTER = 8192
NEG = -1.0e30

ENGINES = ("sync", "gpsimd", "vector", "scalar", "tensor")


class Sem:
    def __init__(self, handle):
        self.h = handle
        self.n = 0

    def bump(self, k=1):
        self.n += k
        return self.n


class Ctx:

    def __init__(self, nc):
        self.nc = nc
        self.q = {e: [] for e in ENGINES}
        self.sems = {}
        self.dsems = []

    def sem(self, name):
        return self.sems[name]

    def op(self, eng, fn):
        self.q[eng].append(fn)

    def dma(self, qeng, out, in_, sem, slow=False):
        tgt = sem.bump(16)
        self.op(qeng, lambda e, o=out, i=in_, s=sem.h, sl=slow: e.dma_start(
            o, i, allow_slow_non_contiguous=sl).then_inc(s, 16))
        return tgt

    def wait(self, eng, sem, val):
        self.op(eng, lambda e, s=sem.h, v=val: e.wait_ge(s, v))

    def barrier(self, wait_map=None):
        bar = self.sems["bar"]
        tgt = bar.n + len(ENGINES)
        bar.n = tgt
        for e in ENGINES:
            if e == "sync":
                for s in self.dsems:
                    if s.n:
                        self.wait(e, s, s.n)
            for (s, v) in (wait_map or {}).get(e, []):
                self.wait(e, s, v)
            self.op(e, lambda eng, s=bar.h: eng.nop().then_inc(s, 1))
            self.wait(e, bar, tgt)


def _ceil(a, b):
    return (a + b - 1) // b



def build_program(dbg=(), race_check=False):
    nc = bass.Bass("TRN2", target_bir_lowering=False,
                   detect_race_conditions=race_check)
    ctx = Ctx(nc)

    def dram_in(name, shape, dtype=F32):
        return nc.dram_tensor(name, shape, dtype, kind="ExternalInput").ap()

    def dram_scratch(name, shape, dtype=F32):
        kind = "ExternalOutput" if name in dbg else "Internal"
        return nc.dram_tensor(name, shape, dtype, kind=kind).ap()

    x0 = dram_in("x0", [T, D])
    inW = dram_in("inW", [D, EIN])
    outW = dram_in("outW", [DIN, D])
    gateW = dram_in("gateW", [D, INTER])
    upW = dram_in("upW", [D, INTER])
    downW = dram_in("downW", [INTER, D])
    w1W = dram_in("w1W", [D, INTER])
    w2W = dram_in("w2W", [D, INTER])
    w3W = dram_in("w3W", [INTER, D])
    wqW = dram_in("wqW", [D, D])
    wkW = dram_in("wkW", [D, D])
    wvW = dram_in("wvW", [D, D])
    woW = dram_in("woW", [D, D])
    wgW = dram_in("wgW", [D, INTER])
    wuW = dram_in("wuW", [D, INTER])
    wdW = dram_in("wdW", [INTER, D])
    convw = dram_in("convw", [128, 34, 4])
    convb = dram_in("convb", [128, 34])
    dtbias_c = dram_in("dtbias_c", [64, 1])
    alog_b = dram_in("alog_b", [128, 64])
    dpar_b = dram_in("dpar_b", [128, 64])
    lam_b = dram_in("lam_b", [128, 16])
    ln1w_c = dram_in("ln1w_c", [128, 16])
    ln2w_row = dram_in("ln2w_row", [128, D])
    normw_col = dram_in("normw_col", [128, 32])
    g1c = dram_in("g1c", [128, 16])
    g1row = dram_in("g1row", [128, D])
    g2row = dram_in("g2row", [128, D])
    g2c = dram_in("g2c", [128, 16])
    g3c = dram_in("g3c", [128, 64])
    idf32_d = dram_in("idf32", [128, 128])
    idbf_d = dram_in("idbf", [128, 128], BF16)
    utones_d = dram_in("utones", [128, 128], BF16)
    allones_d = dram_in("allones", [128, 128], BF16)
    lmask_d = dram_in("lmask", [128, 128])
    cmask_d = dram_in("cmask", [128, 4, 512])

    out = nc.dram_tensor("out", [T, D], F32, kind="ExternalOutput").ap()

    XNF = dram_scratch("XNF", [D, T])
    SZF = dram_scratch("SZF", [DIN, T])
    XBCA = dram_scratch("XBCA", [CONVD, T])
    YNH = dram_scratch("YNH", [DIN, T], BF16)
    YNL = dram_scratch("YNL", [DIN, T], BF16)
    HS = dram_scratch("HS", [T, D])
    SB = dram_scratch("SB", [T, D])
    HFM = dram_scratch("HFM", [INTER, T])
    XQ3 = dram_scratch("XQ3", [INTER, T], BF16)
    X1 = dram_scratch("X1", [T, D])
    XN1F = dram_scratch("XN1F", [D, T])
    XQL1 = dram_scratch("XQL1", [D, T], BF16)
    QFH = dram_scratch("QFH", [D, T], BF16)
    QFL = dram_scratch("QFL", [D, T], BF16)
    KFH = dram_scratch("KFH", [D, T], BF16)
    KFL = dram_scratch("KFL", [D, T], BF16)
    VTH = dram_scratch("VTH", [T, D], BF16)
    VTL = dram_scratch("VTL", [T, D], BF16)
    S1B = dram_scratch("S1B", [T, D])
    H2F = dram_scratch("H2F", [INTER, T])
    XQH2 = dram_scratch("XQH2", [INTER, T], BF16)
    W1Q = dram_scratch("W1Q", [D, INTER], BF16)
    W2Q = dram_scratch("W2Q", [D, INTER], BF16)
    W3Q = dram_scratch("W3Q", [INTER, D], BF16)
    WGQ = dram_scratch("WGQ", [D, INTER], BF16)
    WUQ = dram_scratch("WUQ", [D, INTER], BF16)
    WDQ = dram_scratch("WDQ", [INTER, D], BF16)
    DWH = dram_scratch("DWH", [INTER, D], BF16)
    DWL = dram_scratch("DWL", [INTER, D], BF16)
    WVH = dram_scratch("WVH", [D, D], BF16)
    WVL = dram_scratch("WVL", [D, D], BF16)
    WOH = dram_scratch("WOH", [D, D], BF16)
    WOL = dram_scratch("WOL", [D, D], BF16)
    RB = dram_scratch("RB", [16, 128])
    H1HD = dram_scratch("H1HD", [128, 16, T], BF16)
    H1LD = dram_scratch("H1LD", [128, 16, T], BF16)
    DBGC = dram_scratch("DBGC", [128, 64])

    A = nc.alloc_sbuf_tensor
    idf32 = A("c_idf32", [128, 128], F32).ap()
    idbf = A("c_idbf", [128, 128], BF16).ap()
    utones = A("c_utones", [128, 128], BF16).ap()
    allones = A("c_allones", [128, 128], BF16).ap()
    lmask = A("c_lmask", [128, 128], F32).ap()
    onescol_f = A("c_onescol_f", [128, 1], F32).ap()
    onescol_b = A("c_onescol_b", [128, 1], BF16).ap()
    onesrow_b = A("c_onesrow_b", [1, 128], BF16).ap()
    onesrow_f = A("c_onesrow_f", [1, 128], F32).ap()
    conv_w = A("c_convw", [128, 34, 4], F32).ap()
    conv_b = A("c_convb", [128, 34], F32).ap()
    dtbias = A("c_dtbias", [64, 1], F32).ap()
    negAe = A("c_negAe", [128, 64], F32).ap()
    dparb = A("c_dpar", [128, 64], F32).ap()
    lamsig = A("c_lamsig", [128, 16], F32).ap()
    ln1wc = A("c_ln1wc", [128, 16], F32).ap()
    g1cs = A("c_g1c", [128, 16], F32).ap()
    epsc = A("c_eps", [128, 4], F32).ap()
    g2cs = A("c_g2c", [128, 16], F32).ap()
    g3cs = A("c_g3c", [128, 64], F32).ap()

    def statcol(name):
        return A("st_" + name, [128, NTT], F32).ap()

    negmu = statcol("negmu")
    rstd = statcol("rstd")
    r1c = statcol("r1")
    rq1 = statcol("rq1")
    rq2 = statcol("rq2")
    ry1 = statcol("ry1")
    ry2 = statcol("ry2")
    inv1 = statcol("inv1")
    inv2 = statcol("inv2")
    negmu1 = statcol("negmu1")
    rstd1 = statcol("rstd1")
    invsl1 = statcol("invsl1")
    sl1 = statcol("sl1")
    r2c = statcol("r2")
    msq3 = statcol("msq3")
    m3c = statcol("m3c")
    ry1c = statcol("ry1c")
    ry2c = statcol("ry2c")
    wstat = A("st_wstat", [1, 16], F32).ap()
    thrcol = A("st_thrcol", [128, 12], F32).ap()

    sem_ctxs = []

    def mksem(name):
        cm = nc.semaphore(name)
        h = cm.__enter__()
        sem_ctxs.append(cm)
        s = Sem(h)
        ctx.sems[name] = s
        return s

    s_ld = mksem("s_ld")
    s_lg = mksem("s_lg")
    s_pe = mksem("s_pe")
    s_v = mksem("s_v")
    s_a = mksem("s_a")
    s_gp = mksem("s_gp")
    mksem("bar")
    dx0, dx1 = mksem("dx0"), mksem("dx1")
    dw0, dw1, dw2, dw3 = mksem("dw0"), mksem("dw1"), mksem("dw2"), mksem("dw3")
    dst0, dst1 = mksem("dst0"), mksem("dst1")
    dz, dcs = mksem("dz"), mksem("dcs")
    dr0, dr1 = mksem("dr0"), mksem("dr1")
    dy0, dy1 = mksem("dy0"), mksem("dy1")
    ctx.dsems = [s_ld, s_lg, dx0, dx1, dw0, dw1, dw2, dw3, dst0, dst1,
                 dz, dcs, dr0, dr1, dy0, dy1]

    for dst, src in ((idf32, idf32_d), (idbf, idbf_d), (utones, utones_d),
                     (allones, allones_d), (lmask, lmask_d),
                     (conv_w, convw), (conv_b, convb), (dtbias, dtbias_c),
                     (dparb, dpar_b), (ln1wc, ln1w_c),
                     (g1cs, g1c), (g2cs, g2c), (g3cs, g3c)):
        ctx.dma("sync", dst, src, s_ld)
    ctx.dma("sync", negAe, alog_b, s_ld)
    ctx.dma("sync", lamsig, lam_b, s_ld)
    ld0 = s_ld.n
    ctx.wait("vector", s_ld, ld0)
    ctx.op("vector", lambda e: e.memset(onescol_f, 1.0))
    ctx.op("vector", lambda e: e.memset(onescol_b, 1.0))
    ctx.op("vector", lambda e: e.memset(onesrow_b, 1.0))
    ctx.op("vector", lambda e: e.memset(onesrow_f, 1.0))
    ctx.op("vector", lambda e: e.memset(epsc[:, 0:1], 1e-5))
    ctx.op("vector", lambda e: e.memset(epsc[:, 1:2], 1e-8))
    ctx.op("vector", lambda e: e.memset(epsc[:, 2:3], 1.0))
    ctx.op("vector", lambda e: e.memset(epsc[:, 3:4], 1e-4))
    ctx.wait("scalar", s_ld, ld0)
    ctx.op("scalar", lambda e: e.activation(out=negAe, in_=negAe, func=AF.Exp,
                                            bias=0.0, scale=1.0))
    ctx.op("scalar", lambda e: e.activation(out=lamsig, in_=lamsig, func=AF.Sigmoid,
                                            bias=0.0, scale=1.0).then_inc(s_a.h, 1))
    a_c0 = s_a.bump(0) + 1
    s_a.n = a_c0
    ctx.wait("vector", s_a, a_c0)
    ctx.op("vector", lambda e: e.tensor_scalar_mul(negAe, negAe, -1.0).then_inc(s_v.h, 1))
    s_v.bump()

    ctx.barrier()

    def act(eng_q, out_ap, in_ap, func, bias=0.0, scale=1.0, accum=None, inc=None,
            drain=False):
        if inc is not None:
            inc[0].bump(inc[1])
        def f(e, o=out_ap, i=in_ap, fn=func, b=bias, s=scale, ac=accum, ic=inc,
              dr=drain):
            kw = {}
            if ac is not None:
                kw["accum_out"] = ac
            r = e.activation(out=o, in_=i, func=fn, bias=b, scale=s, **kw)
            if ic is not None:
                r.then_inc(ic[0].h, ic[1])
            if dr:
                e.drain()
        ctx.op(eng_q, f)

    def vop(eng_q, meth, *args, inc=None, drain=False):
        if inc is not None:
            inc[0].bump(inc[1])
        def f(e, m=meth, a=args, ic=inc, dr=drain):
            r = getattr(e, m)(*a)
            if ic is not None:
                r.then_inc(ic[0].h, ic[1])
            if dr:
                e.drain()
        ctx.op(eng_q, f)

    def vtiny(eng_q, meth, *args, inc=None):
        vop(eng_q, meth, *args, inc=inc, drain=True)

    def bcast_ap(col_ap, nfree, blk=None):
        return bass.AP(col_ap.tensor, col_ap.offset,
                       [[col_ap.ap[0][0], 128], [0, nfree]])

    def grp_ap(base_ap, ngrp, grpsz):
        return bass.AP(base_ap.tensor, base_ap.offset,
                       [[base_ap.ap[0][0], 128], [1, ngrp], [0, grpsz]])

    def rne_ladder(eng_q, t_ap, dst_ap, u_ap, c_ap, f_ap, inc=None):
        vop(eng_q, "tensor_scalar", c_ap, t_ap, -1.0, None, ALU.mult)
        vop(eng_q, "tensor_max", u_ap, t_ap, c_ap)
        vop(eng_q, "tensor_scalar", u_ap, u_ap, 0.5, None, ALU.add)
        vop(eng_q, "tensor_copy", f_ap, u_ap)
        for bit in (64.0, 32.0, 16.0, 8.0, 4.0, 2.0, 1.0):
            vop(eng_q, "tensor_scalar", c_ap, u_ap, bit, bit, ALU.is_ge, ALU.mult)
            vop(eng_q, "tensor_sub", u_ap, u_ap, c_ap)
        vop(eng_q, "tensor_sub", f_ap, f_ap, u_ap)
        vop(eng_q, "tensor_scalar", c_ap, t_ap, 0.0, 2.0, ALU.is_ge, ALU.mult)
        vop(eng_q, "scalar_tensor_tensor", c_ap, c_ap, -1.0, f_ap, ALU.add, ALU.mult)
        vop(eng_q, "tensor_scalar", dst_ap, c_ap, 127.0, -128.0, ALU.min, ALU.max,
            inc=inc)

    with (
        nc.sbuf_tensor("pw_buf", [128, 2, 4096], F32) as pw_buf_g,
        nc.sbuf_tensor("pw_acc", [128, 8], F32) as pw_acc_g,
        nc.sbuf_tensor("pw_red", [128, 32], F32) as pw_red_g,
        nc.psum_tensor("pw_ps", [1, 8], F32) as pw_ps_g,
        nc.psum_tensor("pw_ps2", [128, 12], F32) as pw_ps2_g,
    ):
        pw_buf = pw_buf_g.ap()
        pw_acc = pw_acc_g.ap()
        pw_red = pw_red_g.ap()
        pw_ps = pw_ps_g.ap()
        pw_ps2 = pw_ps2_g.ap()

        wlist = [(w1W, D, INTER, 0), (w2W, D, INTER, 1), (w3W, INTER, D, 2),
                 (wgW, D, INTER, 3), (wuW, D, INTER, 4), (wdW, INTER, D, 5)]
        jj = 0
        vmarks = []
        for (wap, rows, cols, idx) in wlist:
            nblk = rows // 128
            first = True
            width = min(cols, 4096)
            for b in range(nblk):
                nw = cols // width
                for wj in range(nw):
                    pb = jj % 2
                    dsl = dw0 if pb == 0 else dw1
                    if jj >= 2:
                        ctx.wait("sync", s_v, vmarks[jj - 2])
                    tgt = ctx.dma("sync", pw_buf[:, pb, 0:width],
                                  wap[b * 128:(b + 1) * 128, wj * width:(wj + 1) * width], dsl)
                    ctx.wait("vector", dsl, tgt)
                    vtiny("vector", "tensor_reduce", pw_red[:, 0:1], pw_buf[:, pb, 0:width],
                          AX.X, ALU.add, True)
                    if first:
                        vtiny("vector", "tensor_copy", pw_acc[:, idx:idx + 1],
                              pw_red[:, 0:1], inc=(s_v, 1))
                        first = False
                    else:
                        vtiny("vector", "tensor_add", pw_acc[:, idx:idx + 1],
                              pw_acc[:, idx:idx + 1], pw_red[:, 0:1], inc=(s_v, 1))
                    vmarks.append(s_v.n)
                    jj += 1
        vq = s_v.n
        ctx.wait("tensor", s_v, vq)
        ctx.op("tensor", lambda e: e.matmul(pw_ps[0:1, 0:6], onescol_f,
                                            pw_acc[:, 0:6],
                                            start=True, stop=True).then_inc(s_pe.h, 1))
        peq = s_pe.bump()
        ctx.wait("vector", s_pe, peq)
        for (_, rows, cols, idx) in wlist:
            vtiny("vector", "tensor_scalar", wstat[0:1, idx:idx + 1],
                pw_ps[0:1, idx:idx + 1], 1.0 / (rows * cols), None, ALU.mult)
        EPS8 = 1e-8
        ATH = 0.5493061443340548
        ATH_HI = float(np.float32(ATH))
        ATH_LO = float(np.float32(ATH - ATH_HI))
        for j, idx in enumerate((0, 1, 2)):
            vtiny("vector", "tensor_scalar", pw_acc[0:1, 0:1],
                wstat[0:1, idx:idx + 1], EPS8, None, ALU.add)
            vtiny("vector", "tensor_scalar", pw_acc[0:1, 1:2],
                pw_acc[0:1, 0:1], ATH_HI, None, ALU.mult)
            vtiny("vector", "tensor_scalar", pw_acc[0:1, 2:3],
                pw_acc[0:1, 0:1], ATH_LO, None, ALU.mult)
            vtiny("vector", "tensor_add", wstat[0:1, 6 + j:7 + j],
                pw_acc[0:1, 1:2], pw_acc[0:1, 2:3])
        for j, idx in enumerate((3, 4, 5)):
            vtiny("vector", "tensor_scalar", wstat[0:1, 9 + j:10 + j],
                wstat[0:1, idx:idx + 1], 1e-5, None, ALU.add)
        for j, idx in enumerate((0, 1, 2)):
            s_ap = wstat[0:1, idx:idx + 1]
            a_ap = wstat[0:1, 12 + j:13 + j]
            vtiny("vector", "tensor_mul", pw_acc[0:1, 0:1], s_ap, s_ap)
            vtiny("vector", "tensor_copy", a_ap, s_ap)
            vtiny("vector", "tensor_copy", pw_acc[0:1, 1:2], s_ap)
            for coef in (1.0 / 3, 0.2, 1.0 / 7, 1.0 / 9):
                vtiny("vector", "tensor_mul", pw_acc[0:1, 1:2], pw_acc[0:1, 1:2],
                    pw_acc[0:1, 0:1])
                vtiny("vector", "tensor_scalar", pw_acc[0:1, 2:3], pw_acc[0:1, 1:2],
                    coef, None, ALU.mult)
                vtiny("vector", "tensor_add", a_ap, a_ap, pw_acc[0:1, 2:3])
        vtiny("vector", "tensor_copy", wstat[0:1, 15:16], wstat[0:1, 14:15], inc=(s_v, 1))
        vq2 = s_v.n
        if "DBGC" in dbg:
            ctx.wait("sync", s_v, vq2)
            ctx.dma("sync", DBGC[:, 32:40], pw_acc, s_ld)
        ctx.wait("tensor", s_v, vq2)
        ctx.op("tensor", lambda e: e.matmul(pw_ps2[:, 0:9], onesrow_f,
                                            wstat[0:1, 6:15],
                                            start=True, stop=True).then_inc(s_pe.h, 1))
        peq2 = s_pe.bump()
        ctx.wait("vector", s_pe, peq2)
        vtiny("vector", "tensor_copy", thrcol[:, 0:9], pw_ps2[:, 0:9], inc=(s_v, 1))

        ctx.barrier()

        tern_list = [(w1W, W1Q, D, INTER, 0), (w2W, W2Q, D, INTER, 1),
                     (w3W, W3Q, INTER, D, 2), (wgW, WGQ, D, INTER, 3),
                     (wuW, WUQ, D, INTER, 4), (wdW, WDQ, INTER, D, 5)]
        with (
            nc.sbuf_tensor("tw_in", [128, 2, 4096], F32) as tw_in_g,
            nc.sbuf_tensor("tw_c1", [128, 4096], F32) as tw_c1_g,
            nc.sbuf_tensor("tw_c2", [128, 4096], F32) as tw_c2_g,
            nc.sbuf_tensor("tw_q", [128, 2, 4096], BF16) as tw_q_g,
            nc.sbuf_tensor("tw_neg", [128, 12], F32) as tw_neg_g,
        ):
            tw_in = tw_in_g.ap()
            tw_c1 = tw_c1_g.ap()
            tw_c2 = tw_c2_g.ap()
            tw_q = tw_q_g.ap()
            tw_neg = tw_neg_g.ap()
            vtiny("vector", "tensor_copy", tw_neg[:, 0:3], thrcol[:, 0:3])
            vtiny("vector", "tensor_scalar", tw_neg[:, 3:6], thrcol[:, 3:6],
                  0.5, None, ALU.mult)
            vtiny("vector", "tensor_scalar", tw_neg[:, 6:12], tw_neg[:, 0:6],
                  -1.0, None, ALU.mult)
            jj = 0
            tmarks = []
            for (wap, qap, rows, cols, idx) in tern_list:
                nblk = rows // 128
                width = min(cols, 4096)
                nw = cols // width
                for b in range(nblk):
                    for wj in range(nw):
                        pb = jj % 2
                        dsl = dw2 if pb == 0 else dw3
                        dss = dw0 if pb == 0 else dw1
                        if jj >= 2:
                            ctx.wait("sync", s_v, tmarks[jj - 2][0])
                        tgt = ctx.dma("sync", tw_in[:, pb, 0:width],
                                      wap[b * 128:(b + 1) * 128,
                                          wj * width:(wj + 1) * width], dsl)
                        ctx.wait("vector", dsl, tgt)
                        if jj >= 2:
                            ctx.wait("vector", dss, tmarks[jj - 2][1])
                        thr1 = tw_neg[:, idx:idx + 1]
                        thrn = tw_neg[:, 6 + idx:7 + idx]
                        vop("vector", "tensor_single_scalar", tw_c1[:, 0:width],
                            tw_in[:, pb, 0:width], thr1, ALU.is_gt)
                        vop("vector", "tensor_single_scalar", tw_c2[:, 0:width],
                            tw_in[:, pb, 0:width], thrn, ALU.is_lt)
                        vop("vector", "tensor_sub", tw_q[:, pb, 0:width],
                            tw_c1[:, 0:width], tw_c2[:, 0:width], inc=(s_v, 1))
                        vdone = s_v.n
                        ctx.wait("sync", s_v, vdone)
                        st = ctx.dma("sync", qap[b * 128:(b + 1) * 128,
                                                 wj * width:(wj + 1) * width],
                                     tw_q[:, pb, 0:width], dss)
                        tmarks.append((vdone, st))
                        jj += 1
            ctx.barrier(wait_map={"sync": [(s_ld, s_ld.n)]})

    if "DBGC" in dbg:
        ctx.dma("sync", DBGC[:, 0:12], thrcol, s_ld)
        ctx.dma("sync", DBGC[0:1, 16:32], wstat, s_ld)
        ctx.wait("sync", s_ld, s_ld.n)
        ctx.barrier()


    with (
        nc.sbuf_tensor("p1_x", [128, 2, D], F32) as p1_x_g,
        nc.sbuf_tensor("p1_xn", [128, D], F32) as p1_xn_g,
        nc.sbuf_tensor("p1_w", [128, D], F32) as p1_w_g,
        nc.sbuf_tensor("p1_c", [128, 8], F32) as p1_c_g,
        nc.sbuf_tensor("p1_g1", [128, D], F32) as p1_g1_g,
        nc.sbuf_tensor("p1_g2", [128, D], F32) as p1_g2_g,
    ):
        p1_x = p1_x_g.ap()
        p1_xn = p1_xn_g.ap()
        p1_w = p1_w_g.ap()
        p1_c = p1_c_g.ap()
        g1rw = p1_g1_g.ap()
        g2rw = p1_g2_g.ap()
        ctx.dma("sync", g1rw, g1row, s_ld)
        grow_tgt = ctx.dma("sync", g2rw, g2row, s_ld)
        ctx.wait("vector", s_ld, grow_tgt)
        ld_marks = []
        a_marks = []
        for tt in range(NTT):
            pb = tt % 2
            dsl = dx0 if pb == 0 else dx1
            if tt >= 2:
                ctx.wait("sync", s_a, a_marks[tt - 2])
            tgt = ctx.dma("sync", p1_x[:, pb], x0[tt * 128:(tt + 1) * 128, :], dsl)
            ld_marks.append(tgt)
            ctx.wait("scalar", dsl, tgt)
            ctx.wait("scalar", s_v, s_v.n)
            act("scalar", p1_w, p1_x[:, pb], AF.Identity, accum=p1_c[:, 0:1])
            act("scalar", p1_w, p1_x[:, pb], AF.Square, accum=p1_c[:, 1:2],
                inc=(s_a, 1), drain=True)
            ctx.wait("vector", s_a, s_a.n)
            cS = lambda j: p1_c[:, j:j + 1]
            ttc = lambda name_ap: name_ap[:, tt:tt + 1]
            vtiny("vector", "tensor_scalar", ttc(negmu), cS(0), -1.0 / D, None, ALU.mult)
            vtiny("vector", "tensor_scalar", cS(2), cS(1), 1.0 / D, None, ALU.mult)
            vtiny("vector", "tensor_mul", cS(3), ttc(negmu), ttc(negmu))
            vtiny("vector", "tensor_sub", cS(2), cS(2), cS(3), inc=(s_v, 1))
            ctx.wait("scalar", s_v, s_v.n)
            act("scalar", cS(4), cS(2), AF.Sqrt, bias=epsc[:, 0:1], scale=1.0,
                inc=(s_a, 1), drain=True)
            ctx.wait("vector", s_a, s_a.n)
            vtiny("vector", "reciprocal", ttc(rstd), cS(4), inc=(s_v, 1))
            vop("vector", "tensor_scalar", p1_xn, p1_x[:, pb], ttc(negmu), ttc(rstd),
                ALU.add, ALU.mult, inc=(s_v, 1))
            ctx.wait("scalar", s_v, s_v.n)
            act("scalar", p1_w, p1_xn, AF.Square, accum=cS(5), inc=(s_a, 1),
                drain=True)
            a_marks.append(s_a.n)
            ctx.wait("vector", s_a, s_a.n)
            vtiny("vector", "tensor_scalar", cS(5), cS(5), 1.0 / D, None, ALU.mult,
                  inc=(s_v, 1))
            ctx.wait("scalar", s_v, s_v.n)
            act("scalar", cS(6), cS(5), AF.Sqrt, bias=epsc[:, 0:1], scale=1.0,
                inc=(s_a, 1), drain=True)
            act("scalar", cS(7), cS(5), AF.Sqrt, bias=epsc[:, 1:2], scale=1.0,
                inc=(s_a, 1), drain=True)
            ctx.wait("vector", s_a, s_a.n)
            vtiny("vector", "reciprocal", ttc(r1c), cS(6))
            vtiny("vector", "reciprocal", cS(7), cS(7))
            for (grow, rq, inv) in ((g1rw, rq1, inv1), (g2rw, rq2, inv2)):
                vop("vector", "tensor_mul", p1_w, p1_xn, grow)
                vtiny("vector", "reduce_max", cS(4), p1_w, AX.X, True)
                vtiny("vector", "tensor_mul", cS(4), cS(4), cS(7))
                vtiny("vector", "tensor_scalar", cS(4), cS(4), 1e-4, None, ALU.max)
                vtiny("vector", "reciprocal", cS(4), cS(4))
                vtiny("vector", "tensor_scalar", cS(4), cS(4), 127.0, None, ALU.mult)
                vtiny("vector", "tensor_mul", ttc(rq), cS(7), cS(4))
                vtiny("vector", "reciprocal", ttc(inv), cS(4), inc=(s_v, 1))
        ctx.barrier(wait_map={"sync": [(s_ld, s_ld.n)]})

    RBROW = dram_scratch("RBROW", [1, INTER])

    def build_rowbcast(cols_ap, ncols, row_sb, dst, ps_bank):
        n = ncols * 128
        vdone = s_v.n
        ctx.wait("tensor", s_v, vdone)
        ctx.op("tensor", lambda e, c=cols_ap, b=ps_bank, nn=ncols: e.transpose(
            b[0:nn, 0:128], c, idf32).then_inc(s_pe.h, 1))
        s_pe.bump()
        ctx.wait("vector", s_pe, s_pe.n)
        vop("vector", "tensor_copy", dst[0:ncols, 0:128], ps_bank[0:ncols, 0:128],
            inc=(s_v, 1))
        ctx.wait("sync", s_v, s_v.n)
        wr = ctx.dma("sync", RBROW.rearrange("o (c p) -> (o c) p", p=128)[0:ncols, :],
                     dst[0:ncols, 0:128], s_ld)
        ctx.wait("sync", s_ld, wr)
        for j in range(_ceil(n, 512)):
            w = min(512, n - j * 512)
            if j > 0:
                ctx.wait("sync", s_pe, s_pe.n)
            tgt = ctx.dma("sync", row_sb[0:1, 0:w],
                          RBROW[0:1, j * 512:j * 512 + w], s_ld)
            ctx.wait("tensor", s_ld, tgt)
            if j > 0:
                ctx.wait("tensor", s_v, s_v.n)
            ctx.op("tensor", lambda e, j=j, w=w, pb=ps_bank: e.matmul(
                pb[:, 0:w], onesrow_f, row_sb[0:1, 0:w],
                start=True, stop=True).then_inc(s_pe.h, 1))
            s_pe.bump()
            ctx.wait("vector", s_pe, s_pe.n)
            vop("vector", "tensor_copy", dst[:, j * 512:j * 512 + w],
                ps_bank[:, 0:w], inc=(s_v, 1))

    import contextlib as _ctl0
    dt_es = _ctl0.ExitStack()
    dt_fm = dt_es.enter_context(
        nc.sbuf_tensor("c_dtfm", [64, T], F32)).ap()
    with (
        nc.sbuf_tensor("p2_h1h", [128, 16, T], BF16) as p2_h1h_g,
        nc.sbuf_tensor("p2_h1l", [128, 16, T], BF16) as p2_h1l_g,
    ):
        h1h = p2_h1h_g.ap()
        h1l = p2_h1l_g.ap()

        with (
            nc.sbuf_tensor("p2_x", [128, 2, D], F32) as p2_x_g,
            nc.sbuf_tensor("p2_xn", [128, 2, D], F32) as p2_xn_g,
            nc.sbuf_tensor("p2_xnf", [128, 16, 128], F32) as p2_xnf_g,
            nc.sbuf_tensor("p2_t1", [128, 2, 128], F32) as p2_t1_g,
            nc.sbuf_tensor("p2_r1b", [128, T], F32) as p2_r1b_g,
            nc.sbuf_tensor("p2_row", [1, 512], F32) as p2_row_g,
            nc.psum_tensor("p2_tp", [128, 2, 512], F32) as p2_tp_g,
        ):
            p2_x = p2_x_g.ap()
            p2_xn = p2_xn_g.ap()
            p2_xnf = p2_xnf_g.ap()
            p2_t1 = p2_t1_g.ap()
            r1b = p2_r1b_g.ap()
            p2_row = p2_row_g.ap()
            tp = p2_tp_g.ap()

            build_rowbcast(r1c, NTT, p2_row, r1b, tp[:, 0, 0:512])
            ctx.barrier()

            bank_ev = {0: (0, 0), 1: (0, 0)}
            x_marks = [0] * NTT
            st_marks = []
            for tt in range(NTT):
                pb = tt % 2
                dsl = dx0 if pb == 0 else dx1
                dss = dst0 if pb == 0 else dst1
                if tt >= 2:
                    ctx.wait("sync", s_v, x_marks[tt - 2])
                if tt >= 2:
                    ctx.wait("sync", dss, st_marks[tt - 2])
                tgt = ctx.dma("sync", p2_x[:, pb], x0[tt * 128:(tt + 1) * 128, :], dsl)
                ctx.wait("vector", dsl, tgt)
                if tt >= 1:
                    pdss = dst0 if (tt - 1) % 2 == 0 else dst1
                    ctx.wait("vector", pdss, st_marks[tt - 1])
                vop("vector", "tensor_scalar", p2_xn[:, pb], p2_x[:, pb],
                    negmu[:, tt:tt + 1], rstd[:, tt:tt + 1], ALU.add, ALU.mult,
                    inc=(s_v, 1))
                x_marks[tt] = s_v.n
                xn_done = s_v.n
                ctx.wait("tensor", s_v, xn_done)
                for dk in range(NDK):
                    bk = dk % 2
                    ctx.wait("tensor", s_v, bank_ev[bk][0])
                    ctx.wait("tensor", s_a, bank_ev[bk][1])
                    ctx.op("tensor", lambda e, pb=pb, dk=dk, bk=bk: e.transpose(
                        tp[:, bk, 0:128], p2_xn[:, pb, dk * 128:(dk + 1) * 128],
                        idf32).then_inc(s_pe.h, 1))
                    s_pe.bump()
                    ctx.wait("vector", s_pe, s_pe.n)
                    ctx.wait("scalar", s_pe, s_pe.n)
                    vop("vector", "tensor_copy", p2_xnf[:, dk], tp[:, bk, 0:128])
                    vop("vector", "scalar_tensor_tensor", p2_t1[:, bk],
                        tp[:, bk, 0:128], ln1wc[:, dk:dk + 1],
                        r1b[:, tt * 128:(tt + 1) * 128], ALU.mult, ALU.mult,
                        inc=(s_v, 1))
                    t1_done = s_v.n
                    ctx.wait("scalar", s_v, t1_done)
                    act("scalar", h1h[:, dk, tt * 128:(tt + 1) * 128],
                        p2_t1[:, bk], AF.Identity, inc=(s_a, 1))
                    ctx.wait("vector", s_a, s_a.n)
                    vop("vector", "tensor_sub", h1l[:, dk, tt * 128:(tt + 1) * 128],
                        p2_t1[:, bk], h1h[:, dk, tt * 128:(tt + 1) * 128],
                        inc=(s_v, 1))
                    bank_ev[bk] = (s_v.n, s_a.n)
                ctx.wait("sync", s_v, bank_ev[0][0])
                ctx.wait("sync", s_v, bank_ev[1][0])
                st = ctx.dma("sync", XNF.rearrange("(dk p) t -> p dk t", p=128)
                             [:, :, tt * 128:(tt + 1) * 128], p2_xnf, dss)
                st_marks.append(st)
            ctx.barrier(wait_map={"sync": [(s_ld, s_ld.n)]})

        if "H1HD" in dbg:
            ctx.dma("sync", H1HD, h1h, s_ld)
            ctx.dma("sync", H1LD, h1l, s_ld)
            ctx.wait("sync", s_ld, s_ld.n)
            ctx.barrier()

        with (
            nc.sbuf_tensor("p2_wf", [128, 2, 16, 128], F32) as p2_wf_g,
            nc.sbuf_tensor("p2_wh", [128, 2, 16, 128], BF16) as p2_wh_g,
            nc.sbuf_tensor("p2_wl", [128, 2, 16, 128], BF16) as p2_wl_g,
            nc.sbuf_tensor("p2_ev", [128, 2, T], F32) as p2_ev_g,
            nc.sbuf_tensor("p2_cv", [128, 2, T], F32) as p2_cv_g,
            nc.psum_tensor("p2_ps", [128, 8, 512], F32) as p2_ps_g,
        ):
            p2_wf = p2_wf_g.ap()
            p2_wh = p2_wh_g.ap()
            p2_wl = p2_wl_g.ap()
            p2_ev = p2_ev_g.ap()
            p2_cv = p2_cv_g.ap()
            ps = p2_ps_g.ap()

            NEC = 67
            inWr = inW.rearrange("(dk p) e -> p dk e", p=128)
            wload = {}
            wprep = {}
            mmdone = {}
            evdone = {}
            stdone = {}

            def w_ec(ec):
                return 64 if ec == 66 else 128

            def issue_load_prep(ec):
                w = w_ec(ec)
                r = ec % 2
                dsl = dw0 if r == 0 else dw1
                if ec >= 2:
                    ctx.wait("sync", s_a, wprep[ec - 2][0])
                    ctx.wait("sync", s_v, wprep[ec - 2][1])
                tgt = ctx.dma("sync", p2_wf[:, r, :, 0:w],
                              inWr[:, :, ec * 128:ec * 128 + w], dsl)
                wload[ec] = tgt
                ctx.wait("scalar", dsl, tgt)
                if ec >= 2:
                    ctx.wait("scalar", s_pe, mmdone[ec - 2])
                act("scalar", p2_wh[:, r, :, 0:w], p2_wf[:, r, :, 0:w],
                    AF.Identity, inc=(s_a, 1))
                ctx.wait("vector", dsl, tgt)
                if ec >= 2:
                    ctx.wait("vector", s_pe, mmdone[ec - 2])
                ctx.wait("vector", s_a, s_a.n)
                vop("vector", "tensor_sub", p2_wl[:, r, :, 0:w],
                    p2_wf[:, r, :, 0:w], p2_wh[:, r, :, 0:w], inc=(s_v, 1))
                wprep[ec] = (s_a.n, s_v.n)

            issue_load_prep(0)
            issue_load_prep(1)

            for ec in range(NEC):
                w = w_ec(ec)
                pr = ec % 2
                qd = pr * 4
                ctx.wait("tensor", s_a, wprep[ec][0])
                ctx.wait("tensor", s_v, wprep[ec][1])
                if ec >= 2:
                    ctx.wait("tensor", s_v, evdone[ec - 2][0])
                    ctx.wait("tensor", s_a, evdone[ec - 2][1])
                wh = p2_wh[:, pr]
                wl = p2_wl[:, pr]

                def mm(bank, lhs, rhs, st, sp, last=False, w=w):
                    def f(e, b=bank, l=lhs, r=rhs, st=st, sp=sp, last=last, w=w):
                        m = e.matmul(ps[0:w, b, 0:512], l, r, start=st, stop=sp)
                        if last:
                            m.then_inc(s_pe.h, 1)
                    ctx.op("tensor", f)

                for dk in range(NDK):
                    for tb in range(NTB):
                        mm(qd + tb, wh[:, dk, 0:w],
                           h1h[:, dk, tb * 512:(tb + 1) * 512], dk == 0, False)
                    for tb in range(NTB):
                        mm(qd + tb, wh[:, dk, 0:w],
                           h1l[:, dk, tb * 512:(tb + 1) * 512], False, False)
                    for tb in range(NTB):
                        last = (dk == NDK - 1)
                        mm(qd + tb, wl[:, dk, 0:w],
                           h1h[:, dk, tb * 512:(tb + 1) * 512], False, last, last)
                        if last:
                            s_pe.bump()
                mmdone[ec] = s_pe.n
                if ec + 2 < NEC:
                    issue_load_prep(ec + 2)
                ctx.wait("scalar", s_pe, mmdone[ec])
                dssb = dst0 if pr == 0 else dst1
                if ec < 32:
                    if ec >= 2:
                        ctx.wait("scalar", dssb, stdone[ec - 2])
                    for tb in range(NTB):
                        act("scalar", p2_ev[:, pr, tb * 512:(tb + 1) * 512],
                            ps[:, qd + tb, 0:512], AF.Silu,
                            inc=(s_a, 1) if tb == NTB - 1 else None)
                    evdone[ec] = (s_v.n, s_a.n)
                    ctx.wait("sync", s_a, s_a.n)
                    stdone[ec] = ctx.dma("sync", SZF[ec * 128:(ec + 1) * 128, :],
                                         p2_ev[:, pr], dssb)
                elif ec < 66:
                    c0 = ec - 32
                    if ec >= 2:
                        ctx.wait("scalar", dssb, stdone[ec - 2])
                    for tb in range(NTB):
                        act("scalar", p2_cv[:, pr, tb * 512:(tb + 1) * 512],
                            ps[:, qd + tb, 0:512], AF.Identity,
                            inc=(s_a, 1) if tb == NTB - 1 else None)
                    cp_done = s_a.n
                    ctx.wait("vector", s_a, cp_done)
                    if ec >= 2:
                        ctx.wait("vector", dssb, stdone[ec - 2])
                    cb = p2_cv[:, pr]
                    eb = p2_ev[:, pr]
                    vop("vector", "tensor_single_scalar", eb, cb,
                        conv_w[:, c0, 3:4], ALU.mult)
                    for (jj, sh) in ((2, 1), (1, 2), (0, 3)):
                        lastc = (jj == 0)
                        vop("vector", "scalar_tensor_tensor", eb[:, sh:T],
                            cb[:, 0:T - sh], conv_w[:, c0, jj:jj + 1], eb[:, sh:T],
                            ALU.mult, ALU.add,
                            inc=(s_v, 1) if lastc else None)
                    cv_done = s_v.n
                    ctx.wait("scalar", s_v, cv_done)
                    act("scalar", eb, eb, AF.Silu, bias=conv_b[:, c0:c0 + 1],
                        scale=1.0, inc=(s_a, 1))
                    evdone[ec] = (s_v.n, s_a.n)
                    ctx.wait("sync", s_a, s_a.n)
                    stdone[ec] = ctx.dma("sync", XBCA[c0 * 128:(c0 + 1) * 128, :],
                                         eb, dssb)
                else:
                    for tb in range(NTB):
                        act("scalar", dt_fm[:, tb * 512:(tb + 1) * 512],
                            ps[0:64, qd + tb, 0:512], AF.Identity,
                            inc=(s_a, 1) if tb == NTB - 1 else None)
                    evdone[ec] = (s_v.n, s_a.n)
                    stdone[ec] = s_ld.n
            ctx.wait("vector", s_a, s_a.n)
            vop("vector", "tensor_single_scalar", dt_fm, dt_fm, dtbias, ALU.add,
                inc=(s_v, 1))
            ctx.wait("scalar", s_v, s_v.n)
            act("scalar", p2_ev[0:64, 0, 0:T], dt_fm, AF.Abs)
            act("scalar", p2_ev[0:64, 0, 0:T], p2_ev[0:64, 0, 0:T], AF.Exp,
                scale=-1.0)
            act("scalar", p2_ev[0:64, 0, 0:T], p2_ev[0:64, 0, 0:T], AF.Ln,
                bias=epsc[0:64, 2:3], scale=1.0)
            act("scalar", p2_ev[0:64, 1, 0:T], dt_fm, AF.Relu, inc=(s_a, 1))
            ctx.wait("vector", s_a, s_a.n)
            vop("vector", "tensor_add", dt_fm, p2_ev[0:64, 0, 0:T],
                p2_ev[0:64, 1, 0:T], inc=(s_v, 1))
            if "DBGC" in dbg:
                ctx.wait("sync", s_v, s_v.n)
                ctx.dma("sync", DBGC[0:64, 0:32], dt_fm[:, 0:32], s_ld)
            ctx.barrier(wait_map={"sync": [(s_ld, s_ld.n)]})


    import contextlib
    with contextlib.ExitStack() as p3_stack:
        g01 = p3_stack.enter_context(nc.sbuf_tensor("p3_hst", [128, 4096], F32))
        g02 = p3_stack.enter_context(nc.sbuf_tensor("p3_hh", [128, 4096], BF16))
        g03 = p3_stack.enter_context(nc.sbuf_tensor("p3_hl", [128, 4096], BF16))
        g04 = p3_stack.enter_context(nc.sbuf_tensor("p3_xbc", [128, 2, 34, 128], F32))
        g05 = p3_stack.enter_context(nc.sbuf_tensor("p3_sz", [128, 32, 128], F32))
        g06 = p3_stack.enter_context(nc.sbuf_tensor("p3_xtm", [128, 4096], F32))
        g07 = p3_stack.enter_context(nc.sbuf_tensor("p3_big", [128, 4096], F32))
        g08 = p3_stack.enter_context(nc.sbuf_tensor("p3_xh", [128, 4096], BF16))
        g09 = p3_stack.enter_context(nc.sbuf_tensor("p3_xl", [128, 4096], BF16))
        g10 = p3_stack.enter_context(nc.sbuf_tensor("p3_dh", [128, 4096], BF16))
        g11 = p3_stack.enter_context(nc.sbuf_tensor("p3_dl", [128, 4096], BF16))
        g12 = p3_stack.enter_context(nc.sbuf_tensor("p3_ysb", [128, 4096], F32))
        g13 = p3_stack.enter_context(nc.sbuf_tensor("p3_tin", [128, 1024], F32))
        g14 = p3_stack.enter_context(nc.sbuf_tensor("p3_tin2", [128, 1024], F32))
        g16 = p3_stack.enter_context(nc.sbuf_tensor("p3_sm", [128, 8, 64], F32))
        g17 = p3_stack.enter_context(nc.sbuf_tensor("p3_bc", [128, 2, 2, 128], BF16))
        g18 = p3_stack.enter_context(nc.sbuf_tensor("p3_btm", [128, 2, 128], BF16))
        g19 = p3_stack.enter_context(nc.sbuf_tensor("p3_g", [128, 128], F32))
        g20 = p3_stack.enter_context(nc.sbuf_tensor("p3_lb", [128, 2, 128], F32))
        g21 = p3_stack.enter_context(nc.sbuf_tensor("p3_el", [128, 2, 128], F32))
        g22 = p3_stack.enter_context(nc.sbuf_tensor("p3_mp", [128, 2, 128], F32))
        g23 = p3_stack.enter_context(nc.sbuf_tensor("p3_mh", [128, 2, 128], BF16))
        g24 = p3_stack.enter_context(nc.sbuf_tensor("p3_ml", [128, 2, 128], BF16))
        g25 = p3_stack.enter_context(nc.sbuf_tensor("p3_dta", [128, 2, 64], BF16))
        g26 = p3_stack.enter_context(nc.sbuf_tensor("p3_row", [1, 3, 512], F32))
        g27 = p3_stack.enter_context(nc.sbuf_tensor("p3_ynh", [128, 16, 128], BF16))
        g28 = p3_stack.enter_context(nc.sbuf_tensor("p3_ynl", [128, 16, 128], BF16))
        g29 = p3_stack.enter_context(nc.psum_tensor("p3_pa", [128, 2, 512], F32))
        g30 = p3_stack.enter_context(nc.psum_tensor("p3_pg", [128, 512], F32))
        g31 = p3_stack.enter_context(nc.psum_tensor("p3_py", [128, 2, 512], F32))
        g32 = p3_stack.enter_context(nc.psum_tensor("p3_ph", [128, 2, 512], F32))
        hst, hh, hl = g01.ap(), g02.ap(), g03.ap()
        xbc, szb, xtm, big = g04.ap(), g05.ap(), g06.ap(), g07.ap()
        xhh, xll, dhh, dll = g08.ap(), g09.ap(), g10.ap(), g11.ap()
        ysb, tin, tin2 = g12.ap(), g13.ap(), g14.ap()
        sm, bcq, btm, gsb = g16.ap(), g17.ap(), g18.ap(), g19.ap()
        lbs, elr, mpr, mhr, mlr = g20.ap(), g21.ap(), g22.ap(), g23.ap(), g24.ap()
        dta, p3row, ynh, ynl = g25.ap(), g26.ap(), g27.ap(), g28.ap()
        pa, pg, py, ph = g29.ap(), g30.ap(), g31.ap(), g32.ap()

        XBCAr = XBCA.rearrange("(cc p) t -> p cc t", p=128)
        SZFr = SZF.rearrange("(cc p) t -> p cc t", p=128)
        YNHr = YNH.rearrange("(cc p) t -> p cc t", p=128)
        YNLr = YNL.rearrange("(cc p) t -> p cc t", p=128)
        DT_, DTA_, CS_, NCS_, ECS_, DEC_, DAT_, CLB_ = range(8)

        def smc(j):
            return sm[:, j]

        PSTR = sm.ap[0][0]

        def sm_grp(slot, qd):
            return bass.AP(sm.tensor, smc(slot).offset + qd * 16,
                           [[PSTR, 128], [1, 16], [0, 64]])

        DSTR = dparb.ap[0][0]

        def dpar_grp(qd):
            return bass.AP(dparb.tensor, dparb.offset + qd * 16,
                           [[DSTR, 128], [1, 16], [0, 64]])

        xbc_marks = {}
        state_v, state_a = 0, 0
        hst_v = 0
        yev_v = 0
        sz_mark = 0
        yn_marks = {}
        yn_amarks = {}
        yn_st_prev = [0, 0]
        csrow_mm = 0

        for c in range(NCH):
            cb = c % 2
            csl = slice(c * 128, (c + 1) * 128)
            dxc = dx0 if cb == 0 else dx1
            if c >= 2:
                ctx.wait("sync", s_v, xbc_marks[c - 2])
            ld_x = ctx.dma("sync", xbc[:, cb], XBCAr[:, :, csl], dxc)
            if c >= 1:
                ctx.wait("sync", s_v, sz_mark)
            ld_z = ctx.dma("sync", szb, SZFr[:, :, csl], dz)

            ctx.wait("tensor", s_v, s_v.n)
            ctx.wait("tensor", s_a, s_a.n)
            ctx.op("tensor", lambda e, c=c: e.transpose(
                pa[0:128, 0, 0:64], dt_fm[:, c * 128:(c + 1) * 128],
                idf32[0:64, 0:64]).then_inc(s_pe.h, 1))
            s_pe.bump()
            ctx.wait("vector", s_pe, s_pe.n)
            vtiny("vector", "tensor_copy", smc(DT_), pa[:, 0, 0:64])
            vtiny("vector", "tensor_mul", smc(DTA_), smc(DT_), negAe)
            vtiny("vector", "tensor_copy", dta[:, 0], smc(DTA_))
            vtiny("vector", "tensor_sub", dta[:, 1], smc(DTA_), dta[:, 0],
                  inc=(s_v, 1))
            dt_split = s_v.n
            ctx.wait("tensor", s_v, dt_split)
            ctx.op("tensor", lambda e: e.matmul(pa[:, 1, 0:64], utones,
                                                dta[:, 0], start=True, stop=False))
            ctx.op("tensor", lambda e: e.matmul(pa[:, 1, 0:64], utones,
                                                dta[:, 1], start=False,
                                                stop=True).then_inc(s_pe.h, 1))
            s_pe.bump()
            ctx.wait("vector", s_pe, s_pe.n)
            vtiny("vector", "tensor_copy", smc(CS_), pa[:, 1, 0:64])
            vtiny("vector", "tensor_scalar", smc(NCS_), smc(CS_), -1.0, None,
                  ALU.mult, inc=(s_v, 1))
            ctx.wait("scalar", s_v, s_v.n)
            act("scalar", smc(ECS_), smc(CS_), AF.Exp, drain=True, inc=(s_a, 1))
            ctx.wait("sync", s_v, s_v.n)
            if c > 0:
                ctx.wait("sync", s_pe, csrow_mm)
            tgt = ctx.dma("sync", p3row[0:1, 2, 0:64], smc(CS_)[127:128, :], dcs)
            ctx.wait("tensor", dcs, tgt)
            ctx.op("tensor", lambda e: e.matmul(pa[:, 0, 0:64], onesrow_f,
                                                p3row[0:1, 2, 0:64], start=True,
                                                stop=True).then_inc(s_pe.h, 1))
            s_pe.bump()
            csrow_mm = s_pe.n
            ctx.wait("vector", s_pe, s_pe.n)
            vtiny("vector", "tensor_copy", smc(CLB_), pa[:, 0, 0:64])
            vtiny("vector", "tensor_sub", smc(DEC_), smc(CLB_), smc(CS_),
                  inc=(s_v, 1))
            ctx.wait("scalar", s_v, s_v.n)
            act("scalar", smc(DEC_), smc(DEC_), AF.Exp, drain=True)
            act("scalar", smc(DAT_), smc(CLB_), AF.Exp, drain=True, inc=(s_a, 1))
            sm_v, sm_a = s_v.n, s_a.n

            ctx.wait("tensor", dxc, ld_x)
            tp_m = {0: (sm_v, sm_a), 1: (sm_v, sm_a)}
            for cc in range(32):
                bk = cc % 2
                ctx.wait("tensor", s_v, tp_m[bk][0])
                ctx.op("tensor", lambda e, cc=cc, bk=bk, cb=cb: e.transpose(
                    pa[:, bk, 0:128], xbc[:, cb, cc], idf32).then_inc(s_pe.h, 1))
                s_pe.bump()
                ctx.wait("vector", s_pe, s_pe.n)
                vop("vector", "tensor_copy", xtm[:, cc * 128:(cc + 1) * 128],
                    pa[:, bk, 0:128], inc=(s_v, 1))
                tp_m[bk] = (s_v.n, 0)
            ctx.wait("scalar", dxc, ld_x)
            act("scalar", bcq[:, 0, 0], xbc[:, cb, 32], AF.Identity)
            act("scalar", bcq[:, 1, 0], xbc[:, cb, 33], AF.Identity, inc=(s_a, 1))
            ctx.wait("vector", s_a, s_a.n)
            vop("vector", "tensor_sub", bcq[:, 0, 1], xbc[:, cb, 32], bcq[:, 0, 0])
            vop("vector", "tensor_sub", bcq[:, 1, 1], xbc[:, cb, 33], bcq[:, 1, 0],
                inc=(s_v, 1))
            bc_v, bc_a = s_v.n, s_a.n
            ctx.wait("tensor", s_v, tp_m[0][0])
            ctx.op("tensor", lambda e, cb=cb: e.transpose(
                pa[:, 0, 0:128], xbc[:, cb, 32], idf32).then_inc(s_pe.h, 1))
            s_pe.bump()
            ctx.wait("vector", s_pe, s_pe.n)
            ctx.wait("scalar", s_pe, s_pe.n)
            act("scalar", btm[:, 0], pa[:, 0, 0:128], AF.Identity, inc=(s_a, 1))
            ctx.wait("vector", s_a, s_a.n)
            vop("vector", "tensor_sub", btm[:, 1], pa[:, 0, 0:128], btm[:, 0],
                inc=(s_v, 1))
            btm_v, btm_a = s_v.n, s_a.n
            xbc_marks[c] = s_v.n

            ctx.wait("scalar", s_v, tp_m[0][0])
            ctx.wait("scalar", s_v, tp_m[1][0])
            act("scalar", xhh, xtm, AF.Identity, inc=(s_a, 1))
            ctx.wait("vector", s_a, s_a.n)
            ctx.wait("vector", s_v, sm_v)
            ctx.wait("vector", s_a, sm_a)
            vop("vector", "tensor_sub", xll, xtm, xhh)
            dt_ap = bass.AP(sm.tensor, smc(DT_).offset,
                            [[PSTR, 128], [1, 64], [0, 64]])
            dec_ap = bass.AP(sm.tensor, smc(DEC_).offset,
                             [[PSTR, 128], [1, 64], [0, 64]])
            xtm3 = xtm.rearrange("p (h q) -> p h q", h=64)
            big3 = big.rearrange("p (h q) -> p h q", h=64)
            vop("vector", "tensor_tensor", big3, xtm3, dt_ap, ALU.mult)
            vop("vector", "tensor_tensor", big3, big3, dec_ap, ALU.mult,
                inc=(s_v, 1))
            ctx.wait("scalar", s_v, s_v.n)
            act("scalar", dhh, big, AF.Identity, inc=(s_a, 1))
            ctx.wait("vector", s_a, s_a.n)
            vop("vector", "tensor_sub", dll, big, dhh, inc=(s_v, 1))
            xp_v, xp_a = s_v.n, s_a.n

            ctx.wait("tensor", s_v, bc_v)
            ctx.wait("tensor", s_a, bc_a)
            ctx.op("tensor", lambda e: e.matmul(pg[:, 0:128], bcq[:, 0, 0],
                                                bcq[:, 1, 0], start=True,
                                                stop=False))
            ctx.op("tensor", lambda e: e.matmul(pg[:, 0:128], bcq[:, 0, 0],
                                                bcq[:, 1, 1], start=False,
                                                stop=False))
            ctx.op("tensor", lambda e: e.matmul(pg[:, 0:128], bcq[:, 0, 1],
                                                bcq[:, 1, 0], start=False,
                                                stop=True).then_inc(s_pe.h, 1))
            s_pe.bump()
            ctx.wait("vector", s_pe, s_pe.n)
            vop("vector", "tensor_copy", gsb, pg[:, 0:128], inc=(s_v, 1))
            g_done = s_v.n

            ctx.wait("tensor", s_v, sm_v)
            ctx.op("tensor", lambda e: e.transpose(
                pa[0:64, 1, 0:128], smc(CS_), idf32).then_inc(s_pe.h, 1))
            s_pe.bump()
            ctx.wait("vector", s_pe, s_pe.n)
            vop("vector", "tensor_copy", lbs[0:64, 0, 0:128], pa[0:64, 1, 0:128],
                inc=(s_v, 1))
            ctx.wait("sync", s_v, s_v.n)
            flat_wr = ctx.dma("sync",
                              RBROW.rearrange("o (c2 p) -> (o c2) p", p=128)
                              [0:64, :], lbs[0:64, 0, 0:128], dw2)
            ctx.wait("sync", dw2, flat_wr)
            pa_free = s_v.n
            row_mm = {}

            intra_marks = {}
            lbq_v = {}
            for qd in range(4):
                hs0 = qd * 1024
                ctx.wait("tensor", s_v, yev_v)
                ctx.wait("tensor", s_v, xp_v)
                ctx.wait("tensor", s_a, xp_a)
                if c > 0:
                    ctx.wait("tensor", s_v, state_v)
                    ctx.wait("tensor", s_a, state_a)
                    for hb in range(2):
                        hsl = slice(hs0 + hb * 512, hs0 + (hb + 1) * 512)
                        ctx.op("tensor", lambda e, hb=hb, hsl=hsl: e.matmul(
                            py[:, hb, 0:512], bcq[:, 1, 0], hh[:, hsl],
                            start=True, stop=False))
                        ctx.op("tensor", lambda e, hb=hb, hsl=hsl: e.matmul(
                            py[:, hb, 0:512], bcq[:, 1, 0], hl[:, hsl],
                            start=False, stop=False))
                        ctx.op("tensor", lambda e, hb=hb, hsl=hsl: e.matmul(
                            py[:, hb, 0:512], bcq[:, 1, 1], hh[:, hsl],
                            start=False, stop=True).then_inc(s_pe.h, 1))
                        s_pe.bump()
                    yint_mm = s_pe.n
                    ctx.wait("vector", s_pe, yint_mm)
                    vop("vector", "tensor_tensor",
                        tin2.rearrange("p (h q) -> p h q", h=16),
                        py.rearrange("p b n -> p (b n)").rearrange(
                            "p (h q) -> p h q", h=16),
                        sm_grp(ECS_, qd), ALU.mult, inc=(s_v, 1))
                    tinter_done = s_v.n
                else:
                    tinter_done = s_v.n
                vop("vector", "tensor_tensor",
                    tin.rearrange("p (h q) -> p h q", h=16),
                    xtm3[:, qd * 16:(qd + 1) * 16], dpar_grp(qd), ALU.mult,
                    inc=(s_v, 1))
                if c > 0:
                    vop("vector", "tensor_add", tin, tin, tin2, inc=(s_v, 1))
                tin_done = s_v.n

                ctx.wait("tensor", s_v, tin_done)
                for hgi in range(4):
                    h0 = qd * 16 + hgi * 4
                    gi = qd * 4 + hgi
                    bk = hgi % 2
                    rr = gi % 2
                    drr = dr0 if rr == 0 else dr1
                    if gi >= 2:
                        ctx.wait("sync", s_pe, row_mm[gi - 2])
                    tgt2 = ctx.dma("sync", p3row[0:1, rr, 0:512],
                                   RBROW[0:1, h0 * 128:(h0 + 4) * 128], drr)
                    ctx.wait("tensor", drr, tgt2)
                    prev = lbq_v.get((qd, hgi - 2)) if hgi >= 2 else (
                        lbq_v.get((qd - 1, 2 + hgi)) if qd > 0 else pa_free)
                    if prev:
                        ctx.wait("tensor", s_v, prev)
                    ctx.op("tensor", lambda e, rr=rr, bk=bk: e.matmul(
                        pa[:, bk, 0:512], onesrow_f,
                        p3row[0:1, rr, 0:512], start=True,
                        stop=True).then_inc(s_pe.h, 1))
                    s_pe.bump()
                    lbmm = s_pe.n
                    row_mm[gi] = s_pe.n
                    for hj in range(4):
                        h = h0 + hj
                        hr = h % 2
                        ctx.wait("vector", s_pe, lbmm)
                        if h >= 2:
                            ctx.wait("vector", s_pe, intra_marks[h - 2])
                            ctx.wait("scalar", s_pe, intra_marks[h - 2])
                        ctx.wait("vector", s_v, g_done)
                        vop("vector", "tensor_tensor", lbs[:, 1, 0:128],
                            pa[:, bk, hj * 128:(hj + 1) * 128], lmask, ALU.add,
                            inc=(s_v, 1))
                        ctx.wait("scalar", s_v, s_v.n)
                        act("scalar", elr[:, hr], lbs[:, 1, 0:128], AF.Exp,
                            bias=smc(NCS_)[:, h:h + 1], scale=1.0, inc=(s_a, 1))
                        ctx.wait("vector", s_a, s_a.n)
                        vtiny("vector", "scalar_tensor_tensor", mpr[:, hr], gsb,
                              smc(DT_)[:, h:h + 1], elr[:, hr], ALU.mult,
                              ALU.mult, inc=(s_v, 1))
                        ctx.wait("scalar", s_v, s_v.n)
                        act("scalar", mhr[:, hr], mpr[:, hr], AF.Identity,
                            drain=True, inc=(s_a, 1))
                        ctx.wait("vector", s_a, s_a.n)
                        vtiny("vector", "tensor_sub", mlr[:, hr], mpr[:, hr],
                              mhr[:, hr], inc=(s_v, 1))
                        mv, ma = s_v.n, s_a.n
                        ctx.wait("tensor", s_v, mv)
                        ctx.wait("tensor", s_a, ma)
                        hof = (h % 16) * 64
                        hb = hof // 512
                        ho = hof % 512
                        fi = (h % 8 == 0)
                        ctx.op("tensor", lambda e, hr=hr, hb=hb, ho=ho, h=h, fi=fi:
                               e.matmul(py[:, hb, ho:ho + 64], mhr[:, hr],
                                        xhh[:, h * 64:(h + 1) * 64],
                                        start=fi, stop=False))
                        ctx.op("tensor", lambda e, hr=hr, hb=hb, ho=ho, h=h:
                               e.matmul(py[:, hb, ho:ho + 64], mhr[:, hr],
                                        xll[:, h * 64:(h + 1) * 64],
                                        start=False, stop=False))
                        lq = (h % 8 == 7)
                        ctx.op("tensor", lambda e, hr=hr, hb=hb, ho=ho, h=h, lq=lq:
                               e.matmul(py[:, hb, ho:ho + 64], mlr[:, hr],
                                        xhh[:, h * 64:(h + 1) * 64],
                                        start=False, stop=lq).then_inc(s_pe.h, 1))
                        s_pe.bump()
                        intra_marks[h] = s_pe.n
                    lbq_v[(qd, hgi)] = s_v.n
                ctx.wait("vector", s_pe, intra_marks[qd * 16 + 15])
                vop("vector", "tensor_add", ysb[:, hs0:hs0 + 512],
                    py[:, 0, 0:512], tin[:, 0:512])
                vop("vector", "tensor_add", ysb[:, hs0 + 512:hs0 + 1024],
                    py[:, 1, 0:512], tin[:, 512:1024], inc=(s_v, 1))
                yev_v = s_v.n
                ctx.wait("tensor", s_v, btm_v)
                ctx.wait("tensor", s_a, btm_a)
                ctx.wait("tensor", s_v, hst_v)
                for hb in range(2):
                    hsl = slice(hs0 + hb * 512, hs0 + (hb + 1) * 512)
                    ctx.op("tensor", lambda e, hb=hb, hsl=hsl: e.matmul(
                        ph[:, hb, 0:512], btm[:, 0], dhh[:, hsl],
                        start=True, stop=False))
                    ctx.op("tensor", lambda e, hb=hb, hsl=hsl: e.matmul(
                        ph[:, hb, 0:512], btm[:, 0], dll[:, hsl],
                        start=False, stop=False))
                    ctx.op("tensor", lambda e, hb=hb, hsl=hsl: e.matmul(
                        ph[:, hb, 0:512], btm[:, 1], dhh[:, hsl],
                        start=False, stop=True).then_inc(s_pe.h, 1))
                    s_pe.bump()
                h_mm = s_pe.n
                ctx.wait("vector", s_pe, h_mm)
                if c > 0:
                    vop("vector", "tensor_tensor",
                        tin2.rearrange("p (h q) -> p h q", h=16),
                        hst.rearrange("p (h q) -> p h q", h=64)
                        [:, qd * 16:(qd + 1) * 16], sm_grp(DAT_, qd), ALU.mult,
                        inc=(s_v, 1))
                    vop("vector", "tensor_add", hst[:, hs0:hs0 + 512],
                        tin2[:, 0:512], ph[:, 0, 0:512])
                    vop("vector", "tensor_add", hst[:, hs0 + 512:hs0 + 1024],
                        tin2[:, 512:1024], ph[:, 1, 0:512], inc=(s_v, 1))
                else:
                    vop("vector", "tensor_copy", hst[:, hs0:hs0 + 512],
                        ph[:, 0, 0:512])
                    vop("vector", "tensor_copy", hst[:, hs0 + 512:hs0 + 1024],
                        ph[:, 1, 0:512], inc=(s_v, 1))
                hst_v = s_v.n
                ctx.wait("scalar", s_v, hst_v)
                act("scalar", hh[:, hs0:hs0 + 1024], hst[:, hs0:hs0 + 1024],
                    AF.Identity, inc=(s_a, 1))
                ctx.wait("vector", s_a, s_a.n)
                vop("vector", "tensor_sub", hl[:, hs0:hs0 + 1024],
                    hst[:, hs0:hs0 + 1024], hh[:, hs0:hs0 + 1024], inc=(s_v, 1))
                state_v, state_a = s_v.n, s_a.n

            ctx.wait("tensor", dz, ld_z)
            ctx.wait("tensor", s_v, state_v)
            tpz = {0: state_v, 1: state_v}
            for cc in range(32):
                bk = cc % 2
                ctx.wait("tensor", s_v, tpz[bk])
                ctx.op("tensor", lambda e, cc=cc, bk=bk: e.transpose(
                    pa[:, bk, 0:128], szb[:, cc], idf32).then_inc(s_pe.h, 1))
                s_pe.bump()
                ctx.wait("vector", s_pe, s_pe.n)
                vop("vector", "tensor_copy", big[:, cc * 128:(cc + 1) * 128],
                    pa[:, bk, 0:128], inc=(s_v, 1))
                tpz[bk] = s_v.n
            sz_mark = s_v.n
            vop("vector", "tensor_mul", ysb, ysb, big, inc=(s_v, 1))
            ctx.wait("scalar", s_v, s_v.n)
            act("scalar", big, ysb, AF.Square, accum=smc(CLB_)[:, 0:1],
                inc=(s_a, 1), drain=True)
            ctx.wait("vector", s_a, s_a.n)
            vtiny("vector", "tensor_scalar", smc(CLB_)[:, 0:1],
                  smc(CLB_)[:, 0:1], 1.0 / DIN, None, ALU.mult, inc=(s_v, 1))
            ctx.wait("scalar", s_v, s_v.n)
            act("scalar", smc(CLB_)[:, 1:2], smc(CLB_)[:, 0:1], AF.Sqrt,
                bias=epsc[:, 0:1], scale=1.0, inc=(s_a, 1), drain=True)
            ctx.wait("vector", s_a, s_a.n)
            vtiny("vector", "reciprocal", smc(CLB_)[:, 2:3], smc(CLB_)[:, 1:2])
            vop("vector", "tensor_single_scalar", ysb, ysb,
                smc(CLB_)[:, 2:3], ALU.mult, inc=(s_v, 1))
            yn_done = s_v.n
            if c > 0:
                ctx.wait("scalar", dy1, yn_st_prev[1])
                ctx.wait("vector", dy1, yn_st_prev[1])
            ctx.wait("tensor", s_v, yn_done)
            if c >= 2:
                ctx.wait("sync", s_v, yn_marks[c - 2])
                ctx.wait("sync", s_a, yn_amarks[c - 2])
            tpy = {0: (yn_done, s_a.n), 1: (yn_done, s_a.n)}
            for cc in range(32):
                bk = cc % 2
                if cc == 16:
                    ctx.wait("scalar", dy0, half_st)
                    ctx.wait("vector", dy0, half_st)
                ctx.wait("tensor", s_v, tpy[bk][0])
                ctx.wait("tensor", s_a, tpy[bk][1])
                ctx.op("tensor", lambda e, cc=cc, bk=bk: e.transpose(
                    pa[:, bk, 0:128], ysb[:, cc * 128:(cc + 1) * 128],
                    idf32).then_inc(s_pe.h, 1))
                s_pe.bump()
                ctx.wait("scalar", s_pe, s_pe.n)
                act("scalar", ynh[:, cc % 16], pa[:, bk, 0:128], AF.Identity,
                    inc=(s_a, 1))
                ctx.wait("vector", s_a, s_a.n)
                vop("vector", "tensor_sub", ynl[:, cc % 16], pa[:, bk, 0:128],
                    ynh[:, cc % 16], inc=(s_v, 1))
                tpy[bk] = (s_v.n, s_a.n)
                if cc == 15 or cc == 31:
                    hf = slice(0, 16) if cc == 15 else slice(16, 32)
                    dyy = dy0 if cc == 15 else dy1
                    if c > 0:
                        ctx.wait("sync", dyy, yn_st_prev[0 if cc == 15 else 1])
                    ctx.wait("sync", s_v, s_v.n)
                    ctx.wait("sync", s_a, s_a.n)
                    ctx.dma("sync", YNHr[:, hf, csl], ynh, dyy)
                    half_st = ctx.dma("sync", YNLr[:, hf, csl], ynl, dyy)
                    if cc == 15:
                        yn_st_cur = [half_st, 0]
                    else:
                        yn_st_cur[1] = half_st
                        yn_st_prev = yn_st_cur
            yn_marks[c] = s_v.n
            yn_amarks[c] = s_a.n
        ctx.barrier(wait_map={"sync": [(s_ld, s_ld.n)]})
    dt_es.close()


    with (
        nc.sbuf_tensor("pwc_in", [128, 2, 2048], F32) as c1g,
        nc.sbuf_tensor("pwc_h", [128, 2, 2048], BF16) as c2g,
        nc.sbuf_tensor("pwc_l", [128, 2, 2048], BF16) as c3g,
    ):
        pwc_in, pwc_h, pwc_l = c1g.ap(), c2g.ap(), c3g.ap()
        marks = {}
        for jj in range(INTER // 128):
            b = jj
            pb = jj % 2
            dsl = dw0 if pb == 0 else dw1
            dss = dw2 if pb == 0 else dw3
            if jj >= 2:
                ctx.wait("sync", s_v, marks[jj - 2][0])
                ctx.wait("sync", s_a, marks[jj - 2][1])
                ctx.wait("sync", dss, marks[jj - 2][2])
            tgt = ctx.dma("sync", pwc_in[:, pb],
                          downW[b * 128:(b + 1) * 128, :], dsl)
            ctx.wait("scalar", dsl, tgt)
            if jj >= 2:
                ctx.wait("scalar", dss, marks[jj - 2][2])
            act("scalar", pwc_h[:, pb], pwc_in[:, pb], AF.Identity, inc=(s_a, 1))
            ctx.wait("vector", dsl, tgt)
            ctx.wait("vector", s_a, s_a.n)
            if jj >= 2:
                ctx.wait("vector", dss, marks[jj - 2][2])
            vop("vector", "tensor_sub", pwc_l[:, pb], pwc_in[:, pb],
                pwc_h[:, pb], inc=(s_v, 1))
            ctx.wait("sync", s_v, s_v.n)
            ctx.wait("sync", s_a, s_a.n)
            ctx.dma("sync", DWH[b * 128:(b + 1) * 128, :], pwc_h[:, pb], dss)
            st = ctx.dma("sync", DWL[b * 128:(b + 1) * 128, :], pwc_l[:, pb], dss)
            marks[jj] = (s_v.n, s_a.n, st)
        ctx.barrier()

    M0 = dram_scratch("M0", [T, D])
    M0F = dram_scratch("M0F", [D, T])
    with (
        nc.sbuf_tensor("pa_wf", [128, 2, 32, 256], F32) as a1g,
        nc.sbuf_tensor("pa_wh", [128, 2, 32, 256], BF16) as a2g,
        nc.sbuf_tensor("pa_wl", [128, 2, 32, 256], BF16) as a3g,
        nc.sbuf_tensor("pa_yh", [128, 2, T], BF16) as a4g,
        nc.sbuf_tensor("pa_yl", [128, 2, T], BF16) as a5g,
        nc.sbuf_tensor("pa_nwc", [128, 32], F32) as a6g,
        nc.sbuf_tensor("pa_mb", [128, 2, 512], F32) as a7g,
        nc.sbuf_tensor("pa_m0", [128, 2, 512], F32) as a8g,
        nc.psum_tensor("pa_ps", [128, 8, 512], F32) as a9g,
    ):
        wfb, whb, wlb = a1g.ap(), a2g.ap(), a3g.ap()
        yhb, ylb = a4g.ap(), a5g.ap()
        nwc = a6g.ap()
        mbuf = a7g.ap()
        m0st = a8g.ap()
        ps4 = a9g.ap()
        ctx.dma("sync", nwc, normw_col, s_ld)
        ctx.wait("scalar", s_ld, s_ld.n)
        ctx.wait("vector", s_ld, s_ld.n)
        M0r = M0.rearrange("(jj p) d -> p jj d", p=128)
        outWr = outW.rearrange("(ik p) d -> p ik d", p=128)
        YNHr2 = YNH.rearrange("(ik p) t -> p ik t", p=128)
        YNLr2 = YNL.rearrange("(ik p) t -> p ik t", p=128)
        wmk = {}

        def mm4(bank, lhs, rhs, start, stop, fin=False):
            def f(e, b=bank, l=lhs, r=rhs, st=start, sp=stop, fi=fin):
                m = e.matmul(ps4[:, b, 0:512], l, r, start=st, stop=sp)
                if fi:
                    m.then_inc(s_pe.h, 1)
            ctx.op("tensor", f)
            if fin:
                s_pe.bump()

        def lp(dcp):
            r = dcp % 2
            dsl = dw0 if r == 0 else dw1
            if dcp >= 2:
                ctx.wait("sync", s_a, wmk[dcp - 2][0])
                ctx.wait("sync", s_v, wmk[dcp - 2][1])
            tgt = ctx.dma("sync", wfb[:, r],
                          outWr[:, :, dcp * 256:(dcp + 1) * 256], dsl)
            ctx.wait("scalar", dsl, tgt)
            if dcp >= 2:
                ctx.wait("scalar", s_pe, wmk[dcp - 2][2])
            for ik in range(32):
                act("scalar", whb[:, r, ik], wfb[:, r, ik], AF.Identity,
                    scale=nwc[:, ik:ik + 1],
                    inc=(s_a, 1) if ik == 31 else None)
            ctx.wait("vector", dsl, tgt)
            ctx.wait("vector", s_a, s_a.n)
            if dcp >= 2:
                ctx.wait("vector", s_pe, wmk[dcp - 2][2])
            for ik in range(32):
                vop("vector", "scalar_tensor_tensor", wlb[:, r, ik], wfb[:, r, ik],
                    nwc[:, ik:ik + 1], whb[:, r, ik], ALU.mult, ALU.subtract,
                    inc=(s_v, 1) if ik == 31 else None)
            wmk[dcp] = [s_a.n, s_v.n, 0]

        lp(0)
        lp(1)
        ev_done = (0, 0)
        for dcp in range(8):
            r = dcp % 2
            ctx.wait("tensor", s_a, wmk[dcp][0])
            ctx.wait("tensor", s_v, wmk[dcp][1])
            ctx.wait("tensor", s_v, ev_done[0])
            ctx.wait("tensor", s_a, ev_done[1])
            mm_mark = {}
            for ik in range(32):
                slot = ik % 2
                dsl = dz if slot == 0 else dcs
                if ik >= 2:
                    ctx.wait("sync", s_pe, mm_mark[ik - 2])
                elif dcp > 0:
                    ctx.wait("sync", s_pe, prev_mmd)
                t1 = ctx.dma("sync", yhb[:, slot], YNHr2[:, ik, :], dsl)
                t2 = ctx.dma("sync", ylb[:, slot], YNLr2[:, ik, :], dsl)
                ctx.wait("tensor", dsl, t2)
                for two in range(2):
                    qd = two * 4
                    co = two * 128
                    for tb in range(NTB):
                        mm4(qd + tb, whb[:, r, ik, co:co + 128],
                            yhb[:, slot, tb * 512:(tb + 1) * 512], ik == 0, False)
                    for tb in range(NTB):
                        mm4(qd + tb, whb[:, r, ik, co:co + 128],
                            ylb[:, slot, tb * 512:(tb + 1) * 512], False, False)
                    for tb in range(NTB):
                        lastf = (ik == 31)
                        mm4(qd + tb, wlb[:, r, ik, co:co + 128],
                            yhb[:, slot, tb * 512:(tb + 1) * 512], False, lastf,
                            fin=(two == 1 and tb == NTB - 1))
                mm_mark[ik] = s_pe.n
            wmk[dcp][2] = s_pe.n
            prev_mmd = s_pe.n
            if dcp + 2 < 8:
                lp(dcp + 2)
            mmd = s_pe.n
            tp_done = {}
            ev2_done = {}
            st_done = {}
            for b in range(8):
                dc = dcp * 2 + b // 4
                tb = b % 4
                sl2 = b % 2
                dss = dst0 if sl2 == 0 else dst1
                ctx.wait("vector", s_pe, mmd)
                if b >= 2:
                    ctx.wait("vector", s_pe, tp_done[b - 2])
                vop("vector", "tensor_copy", mbuf[:, sl2], ps4[:, b, 0:512],
                    inc=(s_v, 1))
                cp = s_v.n
                if "M0F" in dbg:
                    ctx.wait("sync", s_v, cp)
                    dbg_t = ctx.dma("sync", M0F[dc * 128:(dc + 1) * 128,
                                                tb * 512:(tb + 1) * 512],
                                    mbuf[:, sl2], s_ld)
                    ctx.wait("vector", s_ld, dbg_t)
                ctx.wait("tensor", s_v, cp)
                for j in range(4):
                    fin = (j == 3)
                    def ftp(e, b=b, sl2=sl2, j=j, fi=fin):
                        m = e.matmul(ps4[:, b, j * 128:(j + 1) * 128],
                                     mbuf[:, sl2, j * 128:(j + 1) * 128], idf32,
                                     is_transpose=True, start=(j == 0), stop=fi)
                        if fi:
                            m.then_inc(s_pe.h, 1)
                    ctx.op("tensor", ftp)
                    if fin:
                        s_pe.bump()
                tp_done[b] = s_pe.n
                ctx.wait("scalar", s_pe, s_pe.n)
                if b >= 2:
                    ctx.wait("scalar", dss, st_done[b - 2])
                act("scalar", m0st[:, sl2], ps4[:, b, 0:512], AF.Identity,
                    inc=(s_a, 1))
                ev2_done[b] = s_a.n
                ctx.wait("sync", s_a, s_a.n)
                st_done[b] = ctx.dma(
                    "sync",
                    M0r[:, tb * 4:(tb + 1) * 4, dc * 128:(dc + 1) * 128],
                    m0st[:, sl2].rearrange("p (jj d) -> p jj d", d=128), dss)
            ev_done = (s_v.n, s_a.n)
        ctx.barrier()

    MH = dram_scratch("MH", [INTER, T], BF16)
    ML = dram_scratch("ML", [INTER, T], BF16)
    with (
        nc.sbuf_tensor("p4_h2fh", [128, 16, T], BF16) as hh1g,
        nc.sbuf_tensor("p4_h2fl", [128, 16, T], BF16) as hh2g,
    ):
        h2fh = hh1g.ap()
        h2fl = hh2g.ap()
        with (
            nc.sbuf_tensor("pb_x", [128, 2, D], F32) as b1g,
            nc.sbuf_tensor("pb_m", [128, 2, D], F32) as b2g,
            nc.sbuf_tensor("pb_h", [128, D], F32) as b3g,
            nc.sbuf_tensor("pb_l2", [128, D], F32) as b4g,
            nc.sbuf_tensor("pb_c", [128, 8], F32) as b5g,
            nc.psum_tensor("pb_tp", [128, 2, 512], F32) as b6g,
        ):
            pbx, pbm, pbh = b1g.ap(), b2g.ap(), b3g.ap()
            pbl2, pbc = b4g.ap(), b5g.ap()
            tpb = b6g.ap()
            ctx.dma("sync", pbl2, ln2w_row, s_ld)
            ctx.wait("vector", s_ld, s_ld.n)
            bank_ev = {0: (0, 0), 1: (0, 0)}
            xmk = {}
            smk = {}
            tpe_last = 0
            for tt in range(NTT):
                pb = tt % 2
                dsl = dx0 if pb == 0 else dx1
                dsm = dr0 if pb == 0 else dr1
                dss = dst0 if pb == 0 else dst1
                if tt >= 2:
                    ctx.wait("sync", s_v, xmk[tt - 2])
                t1 = ctx.dma("sync", pbx[:, pb], x0[tt * 128:(tt + 1) * 128, :],
                             dsl)
                t2 = ctx.dma("sync", pbm[:, pb], M0[tt * 128:(tt + 1) * 128, :],
                             dsm)
                ctx.wait("vector", dsl, t1)
                ctx.wait("vector", dsm, t2)
                if tt >= 2:
                    ctx.wait("vector", dss, smk[tt - 2])
                vop("vector", "tensor_scalar", pbx[:, pb], pbx[:, pb],
                    negmu[:, tt:tt + 1], rstd[:, tt:tt + 1], ALU.add, ALU.mult)
                vop("vector", "tensor_add", pbx[:, pb], pbx[:, pb], pbm[:, pb],
                    inc=(s_v, 1))
                xmk[tt] = s_v.n
                ctx.wait("sync", s_v, s_v.n)
                smk[tt] = ctx.dma("sync", HS[tt * 128:(tt + 1) * 128, :],
                                  pbx[:, pb], dss)
                ctx.wait("scalar", s_v, s_v.n)
                ctx.wait("scalar", s_pe, tpe_last)
                ctx.wait("vector", s_pe, tpe_last)
                act("scalar", pbh, pbx[:, pb], AF.Square, accum=pbc[:, 0:1],
                    inc=(s_a, 1), drain=True)
                ctx.wait("vector", s_a, s_a.n)
                vtiny("vector", "tensor_scalar", pbc[:, 0:1], pbc[:, 0:1],
                      1.0 / D, None, ALU.mult, inc=(s_v, 1))
                ctx.wait("scalar", s_v, s_v.n)
                act("scalar", pbc[:, 1:2], pbc[:, 0:1], AF.Sqrt,
                    bias=epsc[:, 0:1], scale=1.0, inc=(s_a, 1), drain=True)
                ctx.wait("vector", s_a, s_a.n)
                vtiny("vector", "reciprocal", pbc[:, 2:3], pbc[:, 1:2])
                vop("vector", "tensor_single_scalar", pbh, pbx[:, pb],
                    pbc[:, 2:3], ALU.mult)
                vop("vector", "tensor_mul", pbh, pbh, pbl2, inc=(s_v, 1))
                h2done = s_v.n
                ctx.wait("tensor", s_v, h2done)
                for dk in range(NDK):
                    bk = dk % 2
                    ctx.wait("tensor", s_v, bank_ev[bk][0])
                    ctx.wait("tensor", s_a, bank_ev[bk][1])
                    ctx.op("tensor", lambda e, dk=dk, bk=bk: e.transpose(
                        tpb[:, bk, 0:128], pbh[:, dk * 128:(dk + 1) * 128],
                        idf32).then_inc(s_pe.h, 1))
                    s_pe.bump()
                    ctx.wait("scalar", s_pe, s_pe.n)
                    act("scalar", h2fh[:, dk, tt * 128:(tt + 1) * 128],
                        tpb[:, bk, 0:128], AF.Identity, inc=(s_a, 1))
                    ctx.wait("vector", s_a, s_a.n)
                    vop("vector", "tensor_sub",
                        h2fl[:, dk, tt * 128:(tt + 1) * 128], tpb[:, bk, 0:128],
                        h2fh[:, dk, tt * 128:(tt + 1) * 128], inc=(s_v, 1))
                    bank_ev[bk] = (s_v.n, s_a.n)
                tpe_last = s_pe.n
            ctx.barrier()

        with (
            nc.sbuf_tensor("pc_wf", [128, 2, 16, 128], F32) as d1g,
            nc.sbuf_tensor("pc_wh", [128, 2, 16, 256], BF16) as d2g,
            nc.sbuf_tensor("pc_wl", [128, 2, 16, 256], BF16) as d3g,
            nc.sbuf_tensor("pc_gs", [128, T], F32) as d4g,
            nc.sbuf_tensor("pc_mh", [128, T], BF16) as d5g,
            nc.sbuf_tensor("pc_ml", [128, T], BF16) as d6g,
            nc.psum_tensor("pc_ps", [128, 8, 512], F32) as d7g,
        ):
            cwf, cwh, cwl = d1g.ap(), d2g.ap(), d3g.ap()
            gsb2, mhb, mlb = d4g.ap(), d5g.ap(), d6g.ap()
            psc = d7g.ap()
            gateWr = gateW.rearrange("(ik p) o -> p ik o", p=128)
            upWr = upW.rearrange("(ik p) o -> p ik o", p=128)
            MHr = MH.rearrange("(oc p) t -> p oc t", p=128)
            MLr = ML.rearrange("(oc p) t -> p oc t", p=128)
            wmk2 = {}

            def mmc(bank, lhs, rhs, start, stop, fin=False):
                def f(e, b=bank, l=lhs, r=rhs, st=start, sp=stop, fi=fin):
                    m = e.matmul(psc[:, b, 0:512], l, r, start=st, stop=sp)
                    if fi:
                        m.then_inc(s_pe.h, 1)
                ctx.op("tensor", f)
                if fin:
                    s_pe.bump()

            def lpc(oc):
                r = oc % 2
                dsl = dw0 if r == 0 else dw1
                for (wsrc, co) in ((gateWr, 0), (upWr, 128)):
                    if co == 0 and oc >= 2:
                        ctx.wait("sync", s_a, wmk2[oc - 2][0])
                        ctx.wait("sync", s_v, wmk2[oc - 2][1])
                    if co == 128:
                        ctx.wait("sync", s_a, s_a.n)
                        ctx.wait("sync", s_v, s_v.n)
                    tgt = ctx.dma("sync", cwf[:, r],
                                  wsrc[:, :, oc * 128:(oc + 1) * 128], dsl)
                    ctx.wait("scalar", dsl, tgt)
                    if co == 0 and oc >= 2:
                        ctx.wait("scalar", s_pe, wmk2[oc - 2][2])
                    act("scalar", cwh[:, r, :, co:co + 128], cwf[:, r],
                        AF.Identity, inc=(s_a, 1))
                    ctx.wait("vector", dsl, tgt)
                    ctx.wait("vector", s_a, s_a.n)
                    if co == 0 and oc >= 2:
                        ctx.wait("vector", s_pe, wmk2[oc - 2][2])
                    vop("vector", "tensor_sub", cwl[:, r, :, co:co + 128],
                        cwf[:, r], cwh[:, r, :, co:co + 128], inc=(s_v, 1))
                wmk2[oc] = [s_a.n, s_v.n, 0]

            lpc(0)
            lpc(1)
            prev_ev = (0, 0)
            st_m = {}
            for oc in range(64):
                r = oc % 2
                sl2 = oc % 2
                dss = dst0 if sl2 == 0 else dst1
                ctx.wait("tensor", s_a, wmk2[oc][0])
                ctx.wait("tensor", s_v, wmk2[oc][1])
                ctx.wait("tensor", s_a, prev_ev[1])
                ctx.wait("tensor", s_v, prev_ev[0])
                for (co, qb) in ((0, 0), (128, 4)):
                    for ik in range(NDK):
                        for tb in range(NTB):
                            mmc(qb + tb, cwh[:, r, ik, co:co + 128],
                                h2fh[:, ik, tb * 512:(tb + 1) * 512],
                                ik == 0, False)
                        for tb in range(NTB):
                            mmc(qb + tb, cwh[:, r, ik, co:co + 128],
                                h2fl[:, ik, tb * 512:(tb + 1) * 512],
                                False, False)
                        for tb in range(NTB):
                            lastf = (ik == NDK - 1)
                            mmc(qb + tb, cwl[:, r, ik, co:co + 128],
                                h2fh[:, ik, tb * 512:(tb + 1) * 512],
                                False, lastf, fin=lastf)
                    if co == 0:
                        gmm = s_pe.n
                    else:
                        umm = s_pe.n
                wmk2[oc][2] = s_pe.n
                if oc + 2 < 64:
                    lpc(oc + 2)
                ctx.wait("scalar", s_pe, gmm)
                if oc >= 2:
                    ctx.wait("scalar", dss, st_m[oc - 2])
                    ctx.wait("scalar", s_v, prev_ev2)
                for tb in range(NTB):
                    act("scalar", gsb2[:, tb * 512:(tb + 1) * 512],
                        psc[:, tb, 0:512], AF.Silu,
                        inc=(s_a, 1) if tb == NTB - 1 else None)
                gse = s_a.n
                ctx.wait("vector", s_pe, umm)
                ctx.wait("vector", s_a, gse)
                if oc >= 2:
                    ctx.wait("vector", dss, st_m[oc - 2])
                for tb in range(NTB):
                    vop("vector", "tensor_tensor",
                        gsb2[:, tb * 512:(tb + 1) * 512],
                        psc[:, 4 + tb, 0:512],
                        gsb2[:, tb * 512:(tb + 1) * 512], ALU.mult,
                        inc=(s_v, 1) if tb == NTB - 1 else None)
                mdone = s_v.n
                ctx.wait("scalar", s_v, mdone)
                if oc >= 1:
                    ctx.wait("scalar", dst0 if (oc - 1) % 2 == 0 else dst1,
                             st_m[oc - 1])
                act("scalar", mhb, gsb2, AF.Identity, inc=(s_a, 1))
                ctx.wait("vector", s_a, s_a.n)
                if oc >= 1:
                    ctx.wait("vector", dst0 if (oc - 1) % 2 == 0 else dst1,
                             st_m[oc - 1])
                vop("vector", "tensor_sub", mlb, gsb2, mhb, inc=(s_v, 1))
                prev_ev = (mdone, gse)
                prev_ev2 = s_v.n
                ctx.wait("sync", s_v, s_v.n)
                ctx.wait("sync", s_a, s_a.n)
                ctx.dma("sync", MHr[:, oc, :], mhb, dss)
                st_m[oc] = ctx.dma("sync", MLr[:, oc, :], mlb, dss)
            ctx.barrier()

    with (
        nc.sbuf_tensor("pd_m", [128, 2, 2, 256], BF16) as e1g,
        nc.sbuf_tensor("pd_w", [128, 2, 2, D], BF16) as e2g,
        nc.sbuf_tensor("pd_hs", [128, 2, 2, D], F32) as e3g,
        nc.sbuf_tensor("pd_s", [128, 2, 2, D], F32) as e4g,
        nc.psum_tensor("pd_ps", [128, 8, 512], F32) as e5g,
    ):
        dmm, dwm, dhs, dsb = e1g.ap(), e2g.ap(), e3g.ap(), e4g.ap()
        psd = e5g.ap()
        MHr2 = MH.rearrange("(oc p) t -> p oc t", p=128)
        MLr2 = ML.rearrange("(oc p) t -> p oc t", p=128)
        ev_marks = {}
        tp_mmd = {}
        st_pd = {}
        for tpi in range(8):
            t0 = tpi * 2
            tgp = tpi % 2
            dhl = dx0 if tgp == 0 else dx1
            dss = dst0 if tgp == 0 else dst1
            if tpi >= 2:
                ctx.wait("sync", s_v, ev_marks[tpi - 2])
            th = ctx.dma("sync", dhs[:, tgp, 0], HS[t0 * 128:(t0 + 1) * 128, :],
                         dhl)
            th2 = ctx.dma("sync", dhs[:, tgp, 1],
                          HS[(t0 + 1) * 128:(t0 + 2) * 128, :], dhl)
            mm_mark = {}
            for oc in range(64):
                ocr = oc % 2
                dml = dr0 if ocr == 0 else dr1
                dwl_ = dw0 if ocr == 0 else dw1
                if oc >= 2:
                    ctx.wait("sync", s_pe, mm_mark[oc - 2])
                elif tpi > 0:
                    ctx.wait("sync", s_pe, tp_mmd[tpi - 1])
                tm = ctx.dma("sync", dmm[:, ocr, :, 0:128],
                             MHr2[:, oc, t0 * 128:(t0 + 2) * 128]
                             .rearrange("p (t2 c) -> p t2 c", c=128), dml)
                tm2 = ctx.dma("sync", dmm[:, ocr, :, 128:256],
                              MLr2[:, oc, t0 * 128:(t0 + 2) * 128]
                              .rearrange("p (t2 c) -> p t2 c", c=128), dml)
                tw = ctx.dma("sync", dwm[:, ocr, 0],
                             DWH[oc * 128:(oc + 1) * 128, :], dwl_)
                tw2 = ctx.dma("sync", dwm[:, ocr, 1],
                              DWL[oc * 128:(oc + 1) * 128, :], dwl_)
                ctx.wait("tensor", dml, tm2)
                ctx.wait("tensor", dwl_, tw2)
                if oc == 0:
                    ctx.wait("tensor", s_v, ev_pd if tpi > 0 else 0)
                for t2 in range(2):
                    qd = t2 * 4
                    for db in range(NTB):
                        def fmm(e, ocr=ocr, t2=t2, qd=qd, db=db, st=(oc == 0)):
                            e.matmul(psd[:, qd + db, 0:512],
                                     dmm[:, ocr, t2, 0:128],
                                     dwm[:, ocr, 0, db * 512:(db + 1) * 512],
                                     start=st, stop=False)
                        ctx.op("tensor", fmm)
                    for db in range(NTB):
                        def fmm2(e, ocr=ocr, t2=t2, qd=qd, db=db):
                            e.matmul(psd[:, qd + db, 0:512],
                                     dmm[:, ocr, t2, 0:128],
                                     dwm[:, ocr, 1, db * 512:(db + 1) * 512],
                                     start=False, stop=False)
                        ctx.op("tensor", fmm2)
                    for db in range(NTB):
                        lastf = (oc == 63)
                        fin = (t2 == 1 and db == NTB - 1)
                        def fmm3(e, ocr=ocr, t2=t2, qd=qd, db=db, lf=lastf,
                                 fi=fin):
                            m = e.matmul(psd[:, qd + db, 0:512],
                                         dmm[:, ocr, t2, 128:256],
                                         dwm[:, ocr, 0, db * 512:(db + 1) * 512],
                                         start=False, stop=lf)
                            if fi:
                                m.then_inc(s_pe.h, 1)
                        ctx.op("tensor", fmm3)
                        if fin:
                            s_pe.bump()
                mm_mark[oc] = s_pe.n
            mmd = s_pe.n
            tp_mmd[tpi] = mmd
            ctx.wait("vector", s_pe, mmd)
            ctx.wait("vector", dhl, th2)
            if tpi >= 2:
                ctx.wait("vector", dss, st_pd[tpi - 2])
            for t2 in range(2):
                for db in range(NTB):
                    vop("vector", "tensor_add",
                        dsb[:, tgp, t2, db * 512:(db + 1) * 512],
                        psd[:, t2 * 4 + db, 0:512],
                        dhs[:, tgp, t2, db * 512:(db + 1) * 512],
                        inc=(s_v, 1) if (t2 == 1 and db == NTB - 1) else None)
            ev_pd = s_v.n
            ev_marks[tpi] = s_v.n
            ctx.wait("sync", s_v, s_v.n)
            ctx.dma("sync", SB[t0 * 128:(t0 + 1) * 128, :], dsb[:, tgp, 0], dss)
            st_pd[tpi] = ctx.dma("sync", SB[(t0 + 1) * 128:(t0 + 2) * 128, :],
                                dsb[:, tgp, 1], dss)
        ctx.barrier()


    import contextlib as _ctl
    print("SBUF remaining at P5 start:", nc.sbuf_bytes_remaining)
    with _ctl.ExitStack() as p5s:
        EN = p5s.enter_context
        rb1 = EN(nc.sbuf_tensor("p5_rb1", [128, T], F32)).ap()
        rb2 = EN(nc.sbuf_tensor("p5_rb2", [128, T], F32)).ap()
        ps5 = EN(nc.psum_tensor("p5_ps", [128, 8, 512], F32)).ap()

        with nc.sbuf_tensor("p5_rowA", [1, 512], F32) as rsg:
            rowsb = rsg.ap()
            build_rowbcast(rq1, NTT, rowsb, rb1, ps5[:, 0, 0:512])
            build_rowbcast(rq2, NTT, rowsb, rb2, ps5[:, 1, 0:512])
        ctx.barrier()
        xqs = _ctl.ExitStack()
        xq1f = xqs.enter_context(nc.sbuf_tensor("p5_xq1",
                                                [128, 16, T], BF16)).ap()
        xq2f = xqs.enter_context(nc.sbuf_tensor("p5_xq2",
                                                [128, 16, T], BF16)).ap()

        with (
            nc.sbuf_tensor("p5_xn", [128, 2, T], F32) as q1g,
            nc.sbuf_tensor("p5_t", [128, T], F32) as q2g,
            nc.sbuf_tensor("p5_u", [128, T], F32) as q3g,
            nc.sbuf_tensor("p5_c", [128, T], F32) as q4g,
        ):
            p5xn = q1g.ap()
            p5t = q2g.ap()
            p5u = q3g.ap()
            p5c = q4g.ap()
            XNFr = XNF.rearrange("(dk p) t -> p dk t", p=128)
            vmk = {}
            for dk in range(NDK):
                pb = dk % 2
                dsl = dx0 if pb == 0 else dx1
                if dk >= 2:
                    ctx.wait("sync", s_v, vmk[dk - 2])
                tgt = ctx.dma("sync", p5xn[:, pb], XNFr[:, dk, :], dsl)
                ctx.wait("vector", dsl, tgt)
                for (gcol, rbx, dst) in ((g1cs, rb1, xq1f), (g2cs, rb2, xq2f)):
                    vop("vector", "scalar_tensor_tensor", p5t, p5xn[:, pb],
                        gcol[:, dk:dk + 1], rbx, ALU.mult, ALU.mult)
                    vop("vector", "tensor_scalar", p5c, p5t, -1.0, None, ALU.mult)
                    vop("vector", "tensor_max", p5u, p5t, p5c)
                    vop("vector", "tensor_scalar", p5u, p5u, 0.5, None, ALU.add)
                    vop("vector", "tensor_copy", p5c, p5u)
                    for bit in (64.0, 32.0, 16.0, 8.0, 4.0, 2.0, 1.0):
                        vop("vector", "tensor_scalar", p5t, p5u, bit, bit,
                            ALU.is_ge, ALU.mult)
                        vop("vector", "tensor_sub", p5u, p5u, p5t)
                    vop("vector", "tensor_sub", p5c, p5c, p5u)
                    vop("vector", "scalar_tensor_tensor", p5t, p5xn[:, pb],
                        gcol[:, dk:dk + 1], rbx, ALU.mult, ALU.mult)
                    vop("vector", "tensor_scalar", p5t, p5t, 0.0, 2.0,
                        ALU.is_ge, ALU.mult)
                    vop("vector", "scalar_tensor_tensor", p5t, p5t, -1.0, p5c,
                        ALU.add, ALU.mult)
                    vop("vector", "tensor_scalar", dst[:, dk], p5t, 127.0,
                        -128.0, ALU.min, ALU.max, inc=(s_v, 1))
                vmk[dk] = s_v.n
            ctx.barrier()

        if "XQD1" in dbg:
            XQD1 = dram_scratch("XQD1", [D, T], BF16)
            XQD2 = dram_scratch("XQD2", [D, T], BF16)
            SCLD = dram_scratch("SCLD", [128, 64])
            ctx.dma("sync", XQD1.rearrange("(dk p) t -> p dk t", p=128), xq1f,
                    s_ld)
            ctx.dma("sync", XQD2.rearrange("(dk p) t -> p dk t", p=128), xq2f,
                    s_ld)
            ctx.dma("sync", SCLD[:, 0:16], rq1, s_ld)
            ctx.dma("sync", SCLD[:, 16:32], inv1, s_ld)
            ctx.dma("sync", SCLD[:, 32:48], rq2, s_ld)
            ctx.dma("sync", SCLD[:, 48:64], inv2, s_ld)
            ctx.barrier()

        vtiny("vector", "tensor_single_scalar", ry1c, inv1, thrcol[:, 6:7],
              ALU.mult)
        vtiny("vector", "tensor_single_scalar", ry2c, inv2, thrcol[:, 7:8],
              ALU.mult, inc=(s_v, 1))
        with nc.sbuf_tensor("p5_rowB", [1, 512], F32) as rsg:
            rowsb = rsg.ap()
            build_rowbcast(ry1c, NTT, rowsb, rb1, ps5[:, 0, 0:512])
            build_rowbcast(ry2c, NTT, rowsb, rb2, ps5[:, 1, 0:512])
        ctx.barrier()

        print("SBUF remaining before sweep1:", nc.sbuf_bytes_remaining)
        with (
            nc.sbuf_tensor("p5_w", [128, 2, 2, 16, 128], BF16) as w1g,
            nc.sbuf_tensor("p5_u1", [128, T], F32) as u1g,
            nc.sbuf_tensor("p5_h", [128, T], F32) as h1g,
            nc.sbuf_tensor("p5_sacc", [128, T], F32) as s1g,
            nc.sbuf_tensor("p5_mx", [128, T], F32) as m1g,
            nc.sbuf_tensor("p5_g3sq", [128, 64], F32) as m2g,
        ):
            wq5 = w1g.ap()
            u1b = u1g.ap()
            hb5 = h1g.ap()
            sacc = s1g.ap()
            mxb = m1g.ap()
            g3sq = m2g.ap()
            vtiny("vector", "tensor_mul", g3sq, g3cs, g3cs, inc=(s_v, 1))
            W1Qr = W1Q.rearrange("(dk p) o -> p dk o", p=128)
            W2Qr = W2Q.rearrange("(dk p) o -> p dk o", p=128)
            HFMr = HFM.rearrange("(oc p) t -> p oc t", p=128)
            wmk5 = {}

            def lp5(oc):
                r = oc % 2
                dsl = dw0 if r == 0 else dw1
                if oc >= 2:
                    ctx.wait("sync", s_pe, wmk5[oc - 2])
                ctx.dma("sync", wq5[:, r, 0], W1Qr[:, :, oc * 128:(oc + 1) * 128],
                        dsl)
                return ctx.dma("sync", wq5[:, r, 1],
                               W2Qr[:, :, oc * 128:(oc + 1) * 128], dsl)

            wld = {0: lp5(0), 1: lp5(1)}
            ev5 = (0, 0)
            st5 = {}
            for oc in range(64):
                r = oc % 2
                dsl = dw0 if r == 0 else dw1
                dss = dst0 if r == 0 else dst1
                ctx.wait("tensor", dsl, wld[oc])
                ctx.wait("tensor", s_v, ev5[0])
                ctx.wait("tensor", s_a, ev5[1])
                for (mi, qb) in ((0, 0), (1, 4)):
                    xqf = xq1f if mi == 0 else xq2f
                    for dk in range(NDK):
                        for tb in range(NTB):
                            fin = (dk == NDK - 1 and tb == NTB - 1)
                            def f5(e, r=r, mi=mi, dk=dk, tb=tb, qb=qb, xqf=xqf,
                                   fi=fin):
                                m = e.matmul(ps5[:, qb + tb, 0:512],
                                             wq5[:, r, mi, dk],
                                             xqf[:, dk, tb * 512:(tb + 1) * 512],
                                             start=(dk == 0), stop=(dk == NDK - 1))
                                if fi:
                                    m.then_inc(s_pe.h, 1)
                            ctx.op("tensor", f5)
                            if fin:
                                s_pe.bump()
                mmd5 = s_pe.n
                wmk5[oc] = mmd5
                if oc + 2 < 64:
                    wld[oc + 2] = lp5(oc + 2)
                y1flat = ps5.rearrange("p b n -> p (b n)")[:, 0:T]
                y2flat = ps5.rearrange("p b n -> p (b n)")[:, T:2 * T]
                ctx.wait("vector", s_pe, mmd5)
                if oc >= 1:
                    pdss = dst0 if (oc - 1) % 2 == 0 else dst1
                    ctx.wait("vector", pdss, st5[oc - 1])
                vop("vector", "tensor_mul", u1b, y1flat, rb1, inc=(s_v, 1))
                ctx.wait("scalar", s_v, s_v.n)
                act("scalar", u1b, u1b, AF.Silu, inc=(s_a, 1))
                ctx.wait("vector", s_a, s_a.n)
                vop("vector", "tensor_mul", hb5, y2flat, rb2)
                vop("vector", "tensor_mul", hb5, hb5, u1b)
                vop("vector", "tensor_tensor", u1b, hb5, hb5, ALU.mult)
                if oc == 0:
                    vop("vector", "tensor_copy", sacc, u1b)
                    vop("vector", "tensor_single_scalar", mxb, u1b,
                        g3sq[:, oc:oc + 1], ALU.mult)
                else:
                    vop("vector", "tensor_add", sacc, sacc, u1b)
                    vop("vector", "scalar_tensor_tensor", mxb, u1b,
                        g3sq[:, oc:oc + 1], mxb, ALU.mult, ALU.max)
                vop("vector", "tensor_single_scalar", u1b, hb5,
                    g3cs[:, oc:oc + 1], ALU.mult, inc=(s_v, 1))
                ev5 = (s_v.n, s_a.n)
                ctx.wait("sync", s_v, s_v.n)
                st5[oc] = ctx.dma("sync", HFMr[:, oc, :], u1b, dss)
            vop("vector", "tensor_copy", mxb[:, 0:1], mxb[:, 0:1], inc=(s_v, 1))
            stm = s_v.n
            ctx.wait("tensor", s_v, stm)
            for tt in range(NTT):
                bk = tt % 2
                for (srcb, colj) in ((sacc, 0), (mxb, 1)):
                    ctx.wait("tensor", s_v, s_v.n)
                    ctx.op("tensor", lambda e, tt=tt, bk=bk, s=srcb: e.transpose(
                        ps5[:, bk, 0:128], s[:, tt * 128:(tt + 1) * 128],
                        idf32).then_inc(s_pe.h, 1))
                    s_pe.bump()
                    ctx.wait("vector", s_pe, s_pe.n)
                    if colj == 0:
                        vtiny("vector", "tensor_reduce", msq3[:, tt:tt + 1],
                              ps5[:, bk, 0:128], AX.X, ALU.add, False,
                              inc=(s_v, 1))
                    else:
                        vtiny("vector", "tensor_reduce", m3c[:, tt:tt + 1],
                              ps5[:, bk, 0:128], AX.X, ALU.max, False,
                              inc=(s_v, 1))
            vtiny("vector", "tensor_scalar", msq3, msq3, 1.0 / INTER, None,
                  ALU.mult, inc=(s_v, 1))
            ctx.wait("scalar", s_v, s_v.n)
            act("scalar", msq3, msq3, AF.Sqrt, bias=epsc[:, 1:2], scale=1.0,
                inc=(s_a, 1), drain=True)
            ctx.wait("vector", s_a, s_a.n)
            vtiny("vector", "reciprocal", msq3, msq3)
            ctx.wait("scalar", s_v, s_v.n)
            act("scalar", m3c, m3c, AF.Sqrt, bias=0.0, scale=1.0, inc=(s_a, 1),
                drain=True)
            ctx.wait("vector", s_a, s_a.n)
            vtiny("vector", "tensor_mul", m3c, m3c, msq3)
            vtiny("vector", "tensor_scalar", m3c, m3c, 1e-4, None, ALU.max)
            vtiny("vector", "reciprocal", m3c, m3c)
            vtiny("vector", "tensor_scalar", m3c, m3c, 127.0, None, ALU.mult)
            vtiny("vector", "tensor_mul", msq3, msq3, m3c)
            vtiny("vector", "reciprocal", m3c, m3c)
            vtiny("vector", "tensor_single_scalar", m3c, m3c,
                  thrcol[:, 8:9], ALU.mult, inc=(s_v, 1))
            ctx.barrier()
        xqs.close()

        with nc.sbuf_tensor("p5_rowC", [1, 512], F32) as rsg:
            rowsb = rsg.ap()
            build_rowbcast(msq3, NTT, rowsb, rb1, ps5[:, 0, 0:512])
        ctx.barrier()
        with (
            nc.sbuf_tensor("p5_hgi", [128, 2, T], F32) as r1g,
            nc.sbuf_tensor("p5_t3", [128, T], F32) as r2g,
            nc.sbuf_tensor("p5_u3", [128, T], F32) as r3g,
            nc.sbuf_tensor("p5_c3", [128, T], F32) as r4g,
            nc.sbuf_tensor("p5_q3", [128, 2, T], BF16) as r5g,
        ):
            hgi = r1g.ap()
            t3 = r2g.ap()
            u3 = r3g.ap()
            c3 = r4g.ap()
            q3 = r5g.ap()
            HFMr = HFM.rearrange("(oc p) t -> p oc t", p=128)
            XQ3r = XQ3.rearrange("(oc p) t -> p oc t", p=128)
            vmk3 = {}
            smk3 = {}
            for oc in range(64):
                pb = oc % 2
                dsl = dx0 if pb == 0 else dx1
                dss = dst0 if pb == 0 else dst1
                if oc >= 2:
                    ctx.wait("sync", s_v, vmk3[oc - 2])
                tgt = ctx.dma("sync", hgi[:, pb], HFMr[:, oc, :], dsl)
                ctx.wait("vector", dsl, tgt)
                if oc >= 2:
                    ctx.wait("vector", dss, smk3[oc - 2])
                vop("vector", "tensor_mul", t3, hgi[:, pb], rb1)
                vop("vector", "tensor_scalar", c3, t3, -1.0, None, ALU.mult)
                vop("vector", "tensor_max", u3, t3, c3)
                vop("vector", "tensor_scalar", u3, u3, 0.5, None, ALU.add)
                vop("vector", "tensor_copy", c3, u3)
                for bit in (64.0, 32.0, 16.0, 8.0, 4.0, 2.0, 1.0):
                    vop("vector", "tensor_scalar", t3, u3, bit, bit,
                        ALU.is_ge, ALU.mult)
                    vop("vector", "tensor_sub", u3, u3, t3)
                vop("vector", "tensor_sub", c3, c3, u3)
                vop("vector", "tensor_mul", t3, hgi[:, pb], rb1)
                vop("vector", "tensor_scalar", t3, t3, 0.0, 2.0, ALU.is_ge,
                    ALU.mult)
                vop("vector", "scalar_tensor_tensor", t3, t3, -1.0, c3,
                    ALU.add, ALU.mult)
                vop("vector", "tensor_scalar", q3[:, pb], t3, 127.0, -128.0,
                    ALU.min, ALU.max, inc=(s_v, 1))
                vmk3[oc] = s_v.n
                ctx.wait("sync", s_v, s_v.n)
                smk3[oc] = ctx.dma("sync", XQ3r[:, oc, :], q3[:, pb], dss)
            ctx.barrier()

        with (
            nc.sbuf_tensor("p5_xq3", [128, 2, 64, 256], BF16) as y1g,
            nc.sbuf_tensor("p5_w3", [128, 2, 2048], BF16) as y2g,
            nc.sbuf_tensor("p5_x0", [128, 2, 2, D], F32) as y3g,
            nc.sbuf_tensor("p5_sb", [128, 2, 2, D], F32) as y4g,
            nc.sbuf_tensor("p5_gx", [128, D], F32) as y5g,
            nc.sbuf_tensor("p5_cc", [128, 8], F32) as y6g,
        ):
            xq3s = y1g.ap()
            w3m = y2g.ap()
            x0b = y3g.ap()
            sbb = y4g.ap()
            gx = y5g.ap()
            ccc = y6g.ap()
            XQ3r2 = XQ3.rearrange("(oc p) t -> p oc t", p=128)
            ev4 = 0
            tp_mm = {}
            stx = {}
            for tpi in range(8):
                t0 = tpi * 2
                tgp = tpi % 2
                dxl = dx0 if tgp == 0 else dx1
                dssx = dst0 if tgp == 0 else dst1
                dmq = dr0 if tgp == 0 else dr1
                if tpi >= 2:
                    ctx.wait("sync", s_pe, tp_mm[tpi - 2])
                tq = ctx.dma("sync", xq3s[:, tgp],
                             XQ3r2[:, :, t0 * 128:(t0 + 2) * 128], dmq)
                tx = ctx.dma("sync", x0b[:, tgp, 0],
                             x0[t0 * 128:(t0 + 1) * 128, :], dxl)
                tx = ctx.dma("sync", x0b[:, tgp, 1],
                             x0[(t0 + 1) * 128:(t0 + 2) * 128, :], dxl)
                ts1 = ctx.dma("sync", sbb[:, tgp, 0],
                              SB[t0 * 128:(t0 + 1) * 128, :], dxl)
                ts2 = ctx.dma("sync", sbb[:, tgp, 1],
                              SB[(t0 + 1) * 128:(t0 + 2) * 128, :], dxl)
                mmk4 = {}
                for oc in range(64):
                    ocr = oc % 2
                    dwl_ = dw0 if ocr == 0 else dw1
                    if oc >= 2:
                        ctx.wait("sync", s_pe, mmk4[oc - 2])
                    elif tpi > 0:
                        ctx.wait("sync", s_pe, tp_mm[tpi - 1])
                    tw = ctx.dma("sync", w3m[:, ocr],
                                 W3Q[oc * 128:(oc + 1) * 128, :], dwl_)
                    ctx.wait("tensor", dwl_, tw)
                    ctx.wait("tensor", dmq, tq)
                    if oc == 0 and tpi > 0:
                        ctx.wait("tensor", s_v, ev4)
                    for t2 in range(2):
                        qd = t2 * 4
                        for db in range(NTB):
                            fin = (t2 == 1 and db == NTB - 1)
                            def f4(e, ocr=ocr, t2=t2, qd=qd, db=db, oc=oc,
                                   tgp=tgp, fi=fin):
                                m = e.matmul(ps5[:, qd + db, 0:512],
                                             xq3s[:, tgp, oc,
                                                  t2 * 128:(t2 + 1) * 128],
                                             w3m[:, ocr,
                                                 db * 512:(db + 1) * 512],
                                             start=(oc == 0), stop=(oc == 63))
                                if fi:
                                    m.then_inc(s_pe.h, 1)
                            ctx.op("tensor", f4)
                            if fin:
                                s_pe.bump()
                    mmk4[oc] = s_pe.n
                tp_mm[tpi] = s_pe.n
                for t2 in range(2):
                    tt = t0 + t2
                    ctx.wait("scalar", s_pe, tp_mm[tpi])
                    ctx.wait("scalar", s_v, s_v.n)
                    for db in range(NTB):
                        act("scalar", gx[:, db * 512:(db + 1) * 512],
                            ps5[:, t2 * 4 + db, 0:512], AF.Silu,
                            scale=m3c[:, tt:tt + 1],
                            inc=(s_a, 1) if db == NTB - 1 else None)
                    ctx.wait("vector", s_a, s_a.n)
                    ctx.wait("vector", dxl, ts2)
                    if tpi >= 2:
                        ctx.wait("vector", dssx, stx[tpi - 2])
                    vop("vector", "tensor_mul", gx, gx, sbb[:, tgp, t2])
                    vop("vector", "tensor_add", gx, gx, x0b[:, tgp, t2],
                        inc=(s_v, 1))
                    x1done = s_v.n
                    ctx.wait("sync", s_v, x1done)
                    stx[tpi] = ctx.dma("sync", X1[tt * 128:(tt + 1) * 128, :],
                                       gx, dssx)
                    ctx.wait("vector", dssx, stx[tpi])
                    ctx.wait("scalar", s_v, x1done)
                    act("scalar", x0b[:, tgp, t2], gx, AF.Identity,
                        accum=ccc[:, 0:1], drain=True)
                    act("scalar", x0b[:, tgp, t2], gx, AF.Square,
                        accum=ccc[:, 1:2], inc=(s_a, 1), drain=True)
                    ctx.wait("vector", s_a, s_a.n)
                    vtiny("vector", "tensor_scalar", negmu1[:, tt:tt + 1],
                          ccc[:, 0:1], -1.0 / D, None, ALU.mult)
                    vtiny("vector", "tensor_scalar", ccc[:, 2:3], ccc[:, 1:2],
                          1.0 / D, None, ALU.mult)
                    vtiny("vector", "tensor_mul", ccc[:, 3:4],
                          negmu1[:, tt:tt + 1], negmu1[:, tt:tt + 1])
                    vtiny("vector", "tensor_sub", ccc[:, 2:3], ccc[:, 2:3],
                          ccc[:, 3:4], inc=(s_v, 1))
                    ctx.wait("scalar", s_v, s_v.n)
                    act("scalar", ccc[:, 4:5], ccc[:, 2:3], AF.Sqrt,
                        bias=epsc[:, 0:1], scale=1.0, inc=(s_a, 1), drain=True)
                    ctx.wait("vector", s_a, s_a.n)
                    vtiny("vector", "reciprocal", rstd1[:, tt:tt + 1],
                          ccc[:, 4:5])
                    vop("vector", "tensor_scalar", gx, gx,
                        negmu1[:, tt:tt + 1], rstd1[:, tt:tt + 1],
                        ALU.add, ALU.mult, inc=(s_v, 1))
                    vtiny("vector", "reduce_max", ccc[:, 5:6], gx, AX.X, True)
                    vtiny("vector", "tensor_scalar", sl1[:, tt:tt + 1],
                          ccc[:, 5:6], 1.0 / 127.0, 1e-5, ALU.mult, ALU.add)
                    vtiny("vector", "reciprocal", invsl1[:, tt:tt + 1],
                          sl1[:, tt:tt + 1], inc=(s_v, 1))
                ev4 = s_v.n
            ctx.barrier()
    with (
        nc.sbuf_tensor("p6_xn1h", [128, 16, T], BF16) as x61g,
        nc.sbuf_tensor("p6_xn1l", [128, 16, T], BF16) as x62g,
    ):
        xn1h = x61g.ap()
        xn1l = x62g.ap()
        with (
            nc.sbuf_tensor("p6_x", [128, 2, D], F32) as a1g,
            nc.sbuf_tensor("p6_xn", [128, 2, D], F32) as a2g,
            nc.psum_tensor("p6_tp", [128, 2, 512], F32) as a3g,
        ):
            p6x = a1g.ap()
            p6xn = a2g.ap()
            tp6 = a3g.ap()
            bank_ev = {0: (0, 0), 1: (0, 0)}
            x_marks = [0] * NTT
            for tt in range(NTT):
                pb = tt % 2
                dsl = dx0 if pb == 0 else dx1
                if tt >= 2:
                    ctx.wait("sync", s_v, x_marks[tt - 2])
                tgt = ctx.dma("sync", p6x[:, pb],
                              X1[tt * 128:(tt + 1) * 128, :], dsl)
                ctx.wait("vector", dsl, tgt)
                vop("vector", "tensor_scalar", p6xn[:, pb], p6x[:, pb],
                    negmu1[:, tt:tt + 1], rstd1[:, tt:tt + 1], ALU.add,
                    ALU.mult, inc=(s_v, 1))
                x_marks[tt] = s_v.n
                ctx.wait("tensor", s_v, s_v.n)
                for dk in range(NDK):
                    bk = dk % 2
                    ctx.wait("tensor", s_v, bank_ev[bk][0])
                    ctx.wait("tensor", s_a, bank_ev[bk][1])
                    ctx.op("tensor", lambda e, pb=pb, dk=dk, bk=bk: e.transpose(
                        tp6[:, bk, 0:128], p6xn[:, pb, dk * 128:(dk + 1) * 128],
                        idf32).then_inc(s_pe.h, 1))
                    s_pe.bump()
                    ctx.wait("scalar", s_pe, s_pe.n)
                    act("scalar", xn1h[:, dk, tt * 128:(tt + 1) * 128],
                        tp6[:, bk, 0:128], AF.Identity, inc=(s_a, 1))
                    ctx.wait("vector", s_pe, s_pe.n)
                    ctx.wait("vector", s_a, s_a.n)
                    vop("vector", "tensor_sub",
                        xn1l[:, dk, tt * 128:(tt + 1) * 128],
                        tp6[:, bk, 0:128],
                        xn1h[:, dk, tt * 128:(tt + 1) * 128], inc=(s_v, 1))
                    bank_ev[bk] = (s_v.n, s_a.n)
            ctx.barrier()

        with (
            nc.sbuf_tensor("p6_rbv", [128, T], F32) as b1g,
            nc.sbuf_tensor("p6_t", [128, T], F32) as b2g,
            nc.sbuf_tensor("p6_u", [128, T], F32) as b3g,
            nc.sbuf_tensor("p6_c", [128, T], F32) as b4g,
            nc.sbuf_tensor("p6_f", [128, T], F32) as b5g,
            nc.sbuf_tensor("p6_q", [128, 2, T], BF16) as b6g,
            nc.psum_tensor("p6_psb", [128, 1, 512], F32) as b7g,
        ):
            rbv = b1g.ap()
            t6 = b2g.ap()
            u6 = b3g.ap()
            c6 = b4g.ap()
            f6 = b5g.ap()
            q6 = b6g.ap()
            psb = b7g.ap()
            with nc.sbuf_tensor("p6_row", [1, 512], F32) as rg:
                build_rowbcast(invsl1, NTT, rg.ap(), rbv, psb[:, 0, 0:512])
            ctx.barrier()
            XQL1r = XQL1.rearrange("(dk p) t -> p dk t", p=128)
            smk6 = {}
            for dk in range(NDK):
                pb = dk % 2
                dss = dst0 if pb == 0 else dst1
                vop("vector", "tensor_add", t6, xn1h[:, dk], xn1l[:, dk])
                vop("vector", "tensor_mul", t6, t6, rbv)
                if dk >= 2:
                    ctx.wait("vector", dss, smk6[dk - 2])
                rne_ladder("vector", t6, q6[:, pb], u6, c6, f6, inc=(s_v, 1))
                ctx.wait("sync", s_v, s_v.n)
                smk6[dk] = ctx.dma("sync", XQL1r[:, dk, :], q6[:, pb], dss)
            ctx.barrier()

        wqWr = wqW.rearrange("(dk p) o -> p dk o", p=128)
        wkWr = wkW.rearrange("(dk p) o -> p dk o", p=128)
        wvWr = wvW.rearrange("(dk p) o -> p dk o", p=128)
        with (
            nc.sbuf_tensor("p6_wf", [128, 2, 16, 128], F32) as c1g,
            nc.sbuf_tensor("p6_wh", [128, 2, 2, 16, 128], BF16) as c2g,
            nc.sbuf_tensor("p6_wl", [128, 2, 2, 16, 128], BF16) as c3g,
            nc.sbuf_tensor("p6_qh", [128, T], BF16) as c4g,
            nc.sbuf_tensor("p6_ql", [128, T], BF16) as c5g,
            nc.sbuf_tensor("p6_kh", [128, T], BF16) as c6g,
            nc.sbuf_tensor("p6_kl", [128, T], BF16) as c7g,
            nc.psum_tensor("p6_ps", [128, 8, 512], F32) as c8g,
        ):
            wf6 = c1g.ap()
            wh6 = c2g.ap()
            wl6 = c3g.ap()
            qh6, ql6, kh6, kl6 = c4g.ap(), c5g.ap(), c6g.ap(), c7g.ap()
            ps6 = c8g.ap()
            wfr = wf6.rearrange("p a b c -> p (a b c)")
            prep_mk = {}
            mm_mk = {}
            ev_mk = {}
            st_mk = {}

            def prep6(oc):
                slot = oc % 2
                dsl = dw0 if slot == 0 else dw1
                if oc >= 1:
                    ctx.wait("sync", s_a, prep_mk[oc - 1][0])
                    ctx.wait("sync", s_v, prep_mk[oc - 1][1])
                ctx.dma("sync", wf6[:, 0], wqWr[:, :, oc * 128:(oc + 1) * 128],
                        dsl)
                t2 = ctx.dma("sync", wf6[:, 1],
                             wkWr[:, :, oc * 128:(oc + 1) * 128], dsl)
                whr = wh6[:, slot].rearrange("p a b c -> p (a b c)")
                wlr = wl6[:, slot].rearrange("p a b c -> p (a b c)")
                ctx.wait("scalar", dsl, t2)
                if oc >= 2:
                    ctx.wait("scalar", s_pe, mm_mk[oc - 2])
                act("scalar", whr, wfr, AF.Identity, inc=(s_a, 1))
                ctx.wait("vector", dsl, t2)
                ctx.wait("vector", s_a, s_a.n)
                if oc >= 2:
                    ctx.wait("vector", s_pe, mm_mk[oc - 2])
                vop("vector", "tensor_sub", wlr, wfr, whr, inc=(s_v, 1))
                prep_mk[oc] = (s_a.n, s_v.n)

            prep6(0)
            for oc in range(16):
                slot = oc % 2
                ctx.wait("tensor", s_a, prep_mk[oc][0])
                ctx.wait("tensor", s_v, prep_mk[oc][1])
                if oc >= 1:
                    ctx.wait("tensor", s_v, ev_mk[oc - 1][0])
                    ctx.wait("tensor", s_a, ev_mk[oc - 1][1])
                for dk in range(NDK):
                    for (mi, b0) in ((0, 0), (1, 4)):
                        wh = wh6[:, slot, mi, dk]
                        wl = wl6[:, slot, mi, dk]
                        for (lhs, rhs, st, sp) in (
                            (wh, xn1h, dk == 0, False),
                            (wh, xn1l, False, False),
                            (wl, xn1h, False, dk == NDK - 1),
                        ):
                            for tb in range(NTB):
                                fin = (sp and tb == NTB - 1)
                                def fmm(e, b=b0 + tb, l=lhs, r=rhs, dk=dk,
                                        tb=tb, st=st, sp=sp, fi=fin):
                                    m = e.matmul(
                                        ps6[:, b, 0:512], l,
                                        r[:, dk, tb * 512:(tb + 1) * 512],
                                        start=st, stop=sp)
                                    if fi:
                                        m.then_inc(s_pe.h, 1)
                                ctx.op("tensor", fmm)
                                if fin:
                                    s_pe.bump()
                mm_mk[oc] = s_pe.n
                if oc + 1 < 16:
                    prep6(oc + 1)
                psq = ps6.rearrange("p b n -> p (b n)")
                ctx.wait("scalar", s_pe, mm_mk[oc])
                if oc >= 1:
                    ctx.wait("scalar", dst0, st_mk[oc - 1][0])
                    ctx.wait("scalar", dr0, st_mk[oc - 1][2])
                act("scalar", qh6, psq[:, 0:T], AF.Identity, inc=(s_a, 1))
                act("scalar", kh6, psq[:, T:2 * T], AF.Identity, inc=(s_a, 1))
                ctx.wait("vector", s_a, s_a.n)
                if oc >= 1:
                    ctx.wait("vector", dst1, st_mk[oc - 1][1])
                    ctx.wait("vector", dr1, st_mk[oc - 1][3])
                vop("vector", "tensor_sub", ql6, psq[:, 0:T], qh6)
                vop("vector", "tensor_sub", kl6, psq[:, T:2 * T], kh6,
                    inc=(s_v, 1))
                ev_mk[oc] = (s_v.n, s_a.n)
                ctx.wait("sync", s_v, s_v.n)
                ctx.wait("sync", s_a, s_a.n)
                r0 = oc * 128
                st_mk[oc] = (
                    ctx.dma("sync", QFH[r0:r0 + 128, :], qh6, dst0),
                    ctx.dma("sync", QFL[r0:r0 + 128, :], ql6, dst1),
                    ctx.dma("sync", KFH[r0:r0 + 128, :], kh6, dr0),
                    ctx.dma("sync", KFL[r0:r0 + 128, :], kl6, dr1),
                )
            ctx.barrier()

        with (
            nc.sbuf_tensor("pv_wf", [128, 16, 128], F32) as d1g,
            nc.sbuf_tensor("pv_wh", [128, 2, 16, 128], BF16) as d2g,
            nc.sbuf_tensor("pv_wl", [128, 2, 16, 128], BF16) as d3g,
            nc.sbuf_tensor("pv_vf", [128, T], F32) as d4g,
            nc.sbuf_tensor("pv_sh", [128, 16, 128], BF16) as d6g,
            nc.sbuf_tensor("pv_sl", [128, 16, 128], BF16) as d7g,
            nc.psum_tensor("pv_ps", [128, 8, 512], F32) as d8g,
        ):
            wfv = d1g.ap()
            whv = d2g.ap()
            wlv = d3g.ap()
            vf6 = d4g.ap()
            vsh, vsl = d6g.ap(), d7g.ap()
            psv = d8g.ap()
            wfvr = wfv.rearrange("p a b -> p (a b)")
            VTHr = VTH.rearrange("(tc p) d -> p tc d", p=128)
            VTLr = VTL.rearrange("(tc p) d -> p tc d", p=128)
            prep_mk = {}
            mm_mk = {}
            ev_mk = {}
            st_mk = {}
            cp_mk = {}

            def prepv(oc):
                slot = oc % 2
                dsl = dw0 if slot == 0 else dw1
                if oc >= 1:
                    ctx.wait("sync", s_a, prep_mk[oc - 1][0])
                    ctx.wait("sync", s_v, prep_mk[oc - 1][1])
                tw = ctx.dma("sync", wfv, wvWr[:, :, oc * 128:(oc + 1) * 128],
                             dsl)
                whr = whv[:, slot].rearrange("p a b -> p (a b)")
                wlr = wlv[:, slot].rearrange("p a b -> p (a b)")
                ctx.wait("scalar", dsl, tw)
                if oc >= 2:
                    ctx.wait("scalar", s_pe, mm_mk[oc - 2])
                act("scalar", whr, wfvr, AF.Identity, inc=(s_a, 1))
                ctx.wait("vector", dsl, tw)
                ctx.wait("vector", s_a, s_a.n)
                if oc >= 2:
                    ctx.wait("vector", s_pe, mm_mk[oc - 2])
                vop("vector", "tensor_sub", wlr, wfvr, whr, inc=(s_v, 1))
                prep_mk[oc] = (s_a.n, s_v.n)

            prepv(0)
            for oc in range(16):
                slot = oc % 2
                ctx.wait("tensor", s_a, prep_mk[oc][0])
                ctx.wait("tensor", s_v, prep_mk[oc][1])
                if oc >= 1:
                    ctx.wait("tensor", s_v, ev_mk[oc - 1][0])
                    ctx.wait("tensor", s_a, ev_mk[oc - 1][1])
                for dk in range(NDK):
                    for (lhs, rhs, st, sp) in (
                        (whv, xn1h, dk == 0, False),
                        (whv, xn1l, False, False),
                        (wlv, xn1h, False, dk == NDK - 1),
                    ):
                        for tb in range(NTB):
                            fin = (sp and tb == NTB - 1)
                            def fmv(e, b=tb, l=lhs, r=rhs, dk=dk, tb=tb,
                                    st=st, sp=sp, fi=fin, slot=slot):
                                m = e.matmul(
                                    psv[:, b, 0:512], l[:, slot, dk],
                                    r[:, dk, tb * 512:(tb + 1) * 512],
                                    start=st, stop=sp)
                                if fi:
                                    m.then_inc(s_pe.h, 1)
                            ctx.op("tensor", fmv)
                            if fin:
                                s_pe.bump()
                mm_mk[oc] = s_pe.n
                if oc + 1 < 16:
                    prepv(oc + 1)
                psq = psv.rearrange("p b n -> p (b n)")
                ctx.wait("scalar", s_pe, mm_mk[oc])
                act("scalar", vf6, psq[:, 0:T], AF.Identity, inc=(s_a, 1))
                ev_mk[oc] = (s_v.n, s_a.n)
                ctx.wait("tensor", s_a, s_a.n)
                if oc >= 1:
                    ctx.wait("scalar", dst0, st_mk[oc - 1][0])
                    ctx.wait("vector", dst1, st_mk[oc - 1][1])
                for tc in range(16):
                    bk = 4 + tc % 2
                    gi = oc * 16 + tc
                    if gi >= 2:
                        ctx.wait("tensor", s_v, cp_mk[gi - 2])
                    def ftv(e, bk=bk, tc=tc):
                        e.matmul(psv[:, bk, 0:128],
                                 vf6[:, tc * 128:(tc + 1) * 128], idf32,
                                 is_transpose=True, start=True,
                                 stop=True).then_inc(s_pe.h, 1)
                    ctx.op("tensor", ftv)
                    s_pe.bump()
                    ctx.wait("scalar", s_pe, s_pe.n)
                    act("scalar", vsh[:, tc], psv[:, bk, 0:128], AF.Identity,
                        inc=(s_a, 1))
                    ctx.wait("vector", s_pe, s_pe.n)
                    ctx.wait("vector", s_a, s_a.n)
                    vop("vector", "tensor_sub", vsl[:, tc], psv[:, bk, 0:128],
                        vsh[:, tc], inc=(s_v, 1))
                    cp_mk[gi] = s_v.n
                ctx.wait("sync", s_a, s_a.n)
                ctx.wait("sync", s_v, s_v.n)
                r0 = oc * 128
                st_mk[oc] = (
                    ctx.dma("sync", VTHr[:, :, r0:r0 + 128], vsh, dst0),
                    ctx.dma("sync", VTLr[:, :, r0:r0 + 128], vsl, dst1),
                )
            ctx.barrier()

    with (
        nc.sbuf_tensor("p7_ofh", [128, 16, T], BF16) as o1g,
        nc.sbuf_tensor("p7_ofl", [128, 16, T], BF16) as o2g,
    ):
        ofh = o1g.ap()
        ofl = o2g.ap()
        with (
            nc.sbuf_tensor("p7_kfh", [128, T], BF16) as e1g,
            nc.sbuf_tensor("p7_kfl", [128, T], BF16) as e2g,
            nc.sbuf_tensor("p7_vth", [128, 16, 128], BF16) as e3g,
            nc.sbuf_tensor("p7_vtl", [128, 16, 128], BF16) as e4g,
            nc.sbuf_tensor("p7_e1", [128, T], F32) as e5g,
            nc.sbuf_tensor("p7_e2", [128, T], F32) as e6g,
            nc.sbuf_tensor("p7_pth", [128, 16, 512], BF16) as e7g,
            nc.sbuf_tensor("p7_ptl", [128, 16, 512], BF16) as e8g,
            nc.sbuf_tensor("p7_qf", [128, 2, 2, 128], BF16) as e9g,
            nc.sbuf_tensor("p7_cm", [128, 4, 512], F32) as e10g,
            nc.sbuf_tensor("p7_cc", [128, 8], F32) as e11g,
            nc.psum_tensor("p7_ps", [128, 8, 512], F32) as e12g,
        ):
            kfh, kfl = e1g.ap(), e2g.ap()
            vth, vtl = e3g.ap(), e4g.ap()
            e1b, e2b = e5g.ap(), e6g.ap()
            pth, ptl = e7g.ap(), e8g.ap()
            qf7 = e9g.ap()
            cmsk = e10g.ap()
            cc7 = e11g.ap()
            ps7 = e12g.ap()
            ps7f = ps7.rearrange("p b n -> p (b n)")
            ctx.dma("sync", cmsk, cmask_d, s_ld)
            ctx.wait("vector", s_ld, s_ld.n)
            VTHr = VTH.rearrange("(tc p) d -> p tc d", p=128)
            VTLr = VTL.rearrange("(tc p) d -> p tc d", p=128)
            kv_mk = {}
            q_mk = {}
            sc_mk = {}
            sm_mk = (0, 0)
            pv_mk = {}
            for h in range(16):
                r0 = h * 128
                if h >= 1:
                    ctx.wait("sync", s_pe, pv_mk[h - 1])
                ctx.dma("sync", kfh, KFH[r0:r0 + 128, :], dy0)
                tk = ctx.dma("sync", kfl, KFL[r0:r0 + 128, :], dy0)
                ctx.dma("sync", vth, VTHr[:, :, r0:r0 + 128], dy1)
                tv = ctx.dma("sync", vtl, VTLr[:, :, r0:r0 + 128], dy1)
                ctx.wait("tensor", dy0, tk)
                ctx.wait("tensor", dy1, tv)
                for g in range(4):
                    for qi in range(4):
                        qc = g * 4 + qi
                        nkb = qc // 4 + 1
                        qslot = qc % 2
                        dsl = dr0 if qslot == 0 else dr1
                        if qc >= 2:
                            ctx.wait("sync", s_pe, sc_mk[qc - 2])
                        ctx.dma("sync", qf7[:, qslot, 0],
                                QFH[r0:r0 + 128, qc * 128:(qc + 1) * 128], dsl)
                        tq = ctx.dma("sync", qf7[:, qslot, 1],
                                     QFL[r0:r0 + 128, qc * 128:(qc + 1) * 128],
                                     dsl)
                        ctx.wait("tensor", dsl, tq)
                        ctx.wait("tensor", s_v, sm_mk[0])
                        ctx.wait("tensor", s_a, sm_mk[1])
                        for kb in range(nkb):
                            k0 = kb * 512
                            for (qi2, kx, st, sp) in (
                                (0, kfh, True, False),
                                (0, kfl, False, False),
                                (1, kfh, False, True),
                            ):
                                fin = sp and kb == nkb - 1
                                def fsc(e, kb=kb, k0=k0, qi2=qi2, kx=kx,
                                        st=st, sp=sp, qslot=qslot, fi=fin):
                                    e.matmul(ps7[:, kb, 0:512],
                                             qf7[0:64, qslot, qi2],
                                             kx[0:64, k0:k0 + 512],
                                             start=st, stop=sp,
                                             tile_position=(0, 0))
                                    m = e.matmul(ps7[:, 4 + kb, 0:512],
                                                 qf7[64:128, qslot, qi2],
                                                 kx[64:128, k0:k0 + 512],
                                                 start=st, stop=sp,
                                                 tile_position=(64, 0))
                                    if fi:
                                        m.then_inc(s_pe.h, 1)
                                ctx.op("tensor", fsc)
                                if fin:
                                    s_pe.bump()
                        sc_mk[qc] = s_pe.n
                        w = nkb * 512
                        v4 = qc % 4
                        ctx.wait("vector", s_pe, sc_mk[qc])
                        vop("vector", "tensor_add", ps7[:, nkb - 1, 0:512],
                            ps7[:, nkb - 1, 0:512], cmsk[:, v4])
                        vop("vector", "tensor_add", ps7[:, 3 + nkb, 0:512],
                            ps7[:, 3 + nkb, 0:512], cmsk[:, v4], inc=(s_v, 1))
                        vtiny("vector", "tensor_reduce", cc7[:, 0:1],
                              ps7f[:, 0:w], AX.X, ALU.max, False)
                        vtiny("vector", "tensor_reduce", cc7[:, 1:2],
                              ps7f[:, 4 * 512:4 * 512 + w], AX.X, ALU.max,
                              False)
                        vtiny("vector", "tensor_scalar", cc7[:, 0:2],
                              cc7[:, 0:2], -0.125, None, ALU.mult,
                              inc=(s_v, 1))
                        ctx.wait("scalar", s_v, s_v.n)
                        act("scalar", e1b[:, 0:w], ps7f[:, 0:w], AF.Exp,
                            bias=cc7[:, 0:1], scale=0.125, inc=(s_a, 1))
                        act("scalar", e2b[:, 0:w], ps7f[:, 4 * 512:4 * 512 + w],
                            AF.Exp, bias=cc7[:, 1:2], scale=0.125,
                            inc=(s_a, 1))
                        ctx.wait("vector", s_a, s_a.n)
                        vtiny("vector", "tensor_reduce", cc7[:, 2:3],
                              e1b[:, 0:w], AX.X, ALU.add, False)
                        vtiny("vector", "reciprocal", cc7[:, 2:3], cc7[:, 2:3])
                        vtiny("vector", "tensor_reduce", cc7[:, 3:4],
                              e2b[:, 0:w], AX.X, ALU.add, False)
                        vtiny("vector", "reciprocal", cc7[:, 3:4], cc7[:, 3:4])
                        vtiny("vector", "tensor_mul", cc7[:, 3:4], cc7[:, 3:4],
                              lamsig[:, h:h + 1])
                        vop("vector", "tensor_single_scalar", e2b[:, 0:w],
                            e2b[:, 0:w], cc7[:, 3:4], ALU.mult)
                        vop("vector", "scalar_tensor_tensor", e1b[:, 0:w],
                            e1b[:, 0:w], cc7[:, 2:3], e2b[:, 0:w], ALU.mult,
                            ALU.subtract, inc=(s_v, 1))
                        pd_done = s_v.n
                        ctx.wait("tensor", s_v, pd_done)
                        tpc = {}
                        for sc in range(qc + 1):
                            bk = sc % 4
                            if sc >= 4:
                                ctx.wait("tensor", s_a, tpc[sc - 4][0])
                                ctx.wait("tensor", s_v, tpc[sc - 4][1])
                            def ftp7(e, bk=bk, sc=sc):
                                e.transpose(ps7[:, bk, 0:128],
                                            e1b[:, sc * 128:(sc + 1) * 128],
                                            idf32).then_inc(s_pe.h, 1)
                            ctx.op("tensor", ftp7)
                            s_pe.bump()
                            ctx.wait("scalar", s_pe, s_pe.n)
                            act("scalar", pth[:, sc, qi * 128:(qi + 1) * 128],
                                ps7[:, bk, 0:128], AF.Identity, inc=(s_a, 1))
                            ctx.wait("vector", s_pe, s_pe.n)
                            ctx.wait("vector", s_a, s_a.n)
                            vop("vector", "tensor_sub",
                                ptl[:, sc, qi * 128:(qi + 1) * 128],
                                ps7[:, bk, 0:128],
                                pth[:, sc, qi * 128:(qi + 1) * 128],
                                inc=(s_v, 1))
                            tpc[sc] = (s_a.n, s_v.n)
                        sm_mk = (s_v.n, s_a.n)
                    nsc = 4 * g + 4
                    ctx.wait("tensor", s_v, sm_mk[0])
                    ctx.wait("tensor", s_a, sm_mk[1])
                    for sc in range(nsc):
                        qv0 = max(0, (sc - 4 * g) * 128)
                        for (vx, px, st, sp) in (
                            (vth, pth, sc == 0, False),
                            (vth, ptl, False, False),
                            (vtl, pth, False, sc == nsc - 1),
                        ):
                            fin = sp and sc == nsc - 1
                            def fpv(e, vx=vx, px=px, sc=sc, st=st, sp=sp,
                                    fi=fin, qv0=qv0):
                                m = e.matmul(ps7[:, 7, qv0:512], vx[:, sc],
                                             px[:, sc, qv0:512], start=st,
                                             stop=sp)
                                if fi:
                                    m.then_inc(s_pe.h, 1)
                            ctx.op("tensor", fpv)
                            if fin:
                                s_pe.bump()
                    pv_mk[h] = s_pe.n
                    ctx.wait("scalar", s_pe, pv_mk[h])
                    act("scalar", ofh[:, h, g * 512:(g + 1) * 512],
                        ps7[:, 7, 0:512], AF.Identity, inc=(s_a, 1))
                    ctx.wait("vector", s_pe, pv_mk[h])
                    ctx.wait("vector", s_a, s_a.n)
                    vop("vector", "tensor_sub",
                        ofl[:, h, g * 512:(g + 1) * 512], ps7[:, 7, 0:512],
                        ofh[:, h, g * 512:(g + 1) * 512], inc=(s_v, 1))
                    sm_mk = (s_v.n, s_a.n)
            ctx.barrier()

        if "OFH" in dbg:
            OFH = dram_scratch("OFH", [128, 16, T], BF16)
            OFL = dram_scratch("OFL", [128, 16, T], BF16)
            ctx.dma("sync", OFH, ofh, s_ld)
            ctx.dma("sync", OFL, ofl, s_ld)
            ctx.wait("sync", s_ld, s_ld.n)
            ctx.barrier()

        woWr = woW.rearrange("(dk p) d -> p dk d", p=128)
        S1Br = S1B.rearrange("(jj p) d -> p jj d", p=128)
        with (
            nc.sbuf_tensor("p7w_wf", [128, 16, 128], F32) as w1g,
            nc.sbuf_tensor("p7w_wh", [128, 2, 16, 128], BF16) as w2g,
            nc.sbuf_tensor("p7w_wl", [128, 2, 16, 128], BF16) as w3g,
            nc.sbuf_tensor("p7w_mb", [128, 2, 512], F32) as w4g,
            nc.sbuf_tensor("p7w_st", [128, 2, 512], F32) as w5g,
            nc.psum_tensor("p7w_ps", [128, 4, 512], F32) as w6g,
        ):
            wfo = w1g.ap()
            who = w2g.ap()
            wlo = w3g.ap()
            mb7 = w4g.ap()
            st7 = w5g.ap()
            psw = w6g.ap()
            wfor = wfo.rearrange("p a b -> p (a b)")
            prep_mk = {}
            mm_mk = {}
            ev_mk = (0, 0)
            st_mk = {}
            tp_mk = {}

            def prepw(oc):
                slot = oc % 2
                dsl = dw0 if slot == 0 else dw1
                if oc >= 1:
                    ctx.wait("sync", s_a, prep_mk[oc - 1][0])
                    ctx.wait("sync", s_v, prep_mk[oc - 1][1])
                tw = ctx.dma("sync", wfo, woWr[:, :, oc * 128:(oc + 1) * 128],
                             dsl)
                whr = who[:, slot].rearrange("p a b -> p (a b)")
                wlr = wlo[:, slot].rearrange("p a b -> p (a b)")
                ctx.wait("scalar", dsl, tw)
                if oc >= 2:
                    ctx.wait("scalar", s_pe, mm_mk[oc - 2])
                act("scalar", whr, wfor, AF.Identity, inc=(s_a, 1))
                ctx.wait("vector", dsl, tw)
                ctx.wait("vector", s_a, s_a.n)
                if oc >= 2:
                    ctx.wait("vector", s_pe, mm_mk[oc - 2])
                vop("vector", "tensor_sub", wlr, wfor, whr, inc=(s_v, 1))
                prep_mk[oc] = (s_a.n, s_v.n)

            prepw(0)
            for oc in range(16):
                slot = oc % 2
                ctx.wait("tensor", s_a, prep_mk[oc][0])
                ctx.wait("tensor", s_v, prep_mk[oc][1])
                ctx.wait("tensor", s_v, ev_mk[0])
                ctx.wait("tensor", s_a, ev_mk[1])
                for dk in range(NDK):
                    for (lhs, rhs, st, sp) in (
                        (who, ofh, dk == 0, False),
                        (who, ofl, False, False),
                        (wlo, ofh, False, dk == NDK - 1),
                    ):
                        for tb in range(NTB):
                            fin = (sp and tb == NTB - 1)
                            def fmw(e, b=tb, l=lhs, r=rhs, dk=dk, tb=tb,
                                    st=st, sp=sp, fi=fin, slot=slot):
                                m = e.matmul(
                                    psw[:, b, 0:512], l[:, slot, dk],
                                    r[:, dk, tb * 512:(tb + 1) * 512],
                                    start=st, stop=sp)
                                if fi:
                                    m.then_inc(s_pe.h, 1)
                            ctx.op("tensor", fmw)
                            if fin:
                                s_pe.bump()
                mm_mk[oc] = s_pe.n
                if oc + 1 < 16:
                    prepw(oc + 1)
                mmd = s_pe.n
                for b in range(4):
                    sl2 = b % 2
                    dss = dst0 if sl2 == 0 else dst1
                    gb = oc * 4 + b
                    ctx.wait("vector", s_pe, mmd)
                    if gb >= 2:
                        ctx.wait("vector", s_pe, tp_mk[gb - 2])
                    vop("vector", "tensor_copy", mb7[:, sl2], psw[:, b, 0:512],
                        inc=(s_v, 1))
                    cp = s_v.n
                    ctx.wait("tensor", s_v, cp)
                    for j in range(4):
                        fin = (j == 3)
                        def ftw(e, b=b, sl2=sl2, j=j, fi=fin):
                            m = e.matmul(psw[:, b, j * 128:(j + 1) * 128],
                                         mb7[:, sl2, j * 128:(j + 1) * 128],
                                         idf32, is_transpose=True,
                                         start=(j == 0), stop=fi)
                            if fi:
                                m.then_inc(s_pe.h, 1)
                        ctx.op("tensor", ftw)
                        if fin:
                            s_pe.bump()
                    tp_mk[gb] = s_pe.n
                    ctx.wait("scalar", s_pe, s_pe.n)
                    if gb >= 2:
                        ctx.wait("scalar", dss, st_mk[gb - 2])
                    act("scalar", st7[:, sl2], psw[:, b, 0:512], AF.Identity,
                        inc=(s_a, 1))
                    ctx.wait("sync", s_a, s_a.n)
                    st_mk[gb] = ctx.dma(
                        "sync",
                        S1Br[:, b * 4:(b + 1) * 4, oc * 128:(oc + 1) * 128],
                        st7[:, sl2].rearrange("p (jj d) -> p jj d", d=128),
                        dss)
                ev_mk = (s_v.n, s_a.n)
            ctx.barrier()

    vtiny("vector", "tensor_single_scalar", ry1c, sl1, thrcol[:, 3:4], ALU.mult)
    vtiny("vector", "tensor_single_scalar", ry2c, sl1, thrcol[:, 4:5], ALU.mult,
          inc=(s_v, 1))
    with _ctl.ExitStack() as p8s:
        E8 = p8s.enter_context
        xql1r = E8(nc.sbuf_tensor("p8_xq", [128, 16, T], BF16)).ap()
        rbg = E8(nc.sbuf_tensor("p8_rbg", [128, T], F32)).ap()
        rbu = E8(nc.sbuf_tensor("p8_rbu", [128, T], F32)).ap()
        ps8 = E8(nc.psum_tensor("p8_ps", [128, 8, 512], F32)).ap()
        with nc.sbuf_tensor("p8_row", [1, 512], F32) as rg:
            rowsb = rg.ap()
            build_rowbcast(ry1c, NTT, rowsb, rbg, ps8[:, 0, 0:512])
            build_rowbcast(ry2c, NTT, rowsb, rbu, ps8[:, 1, 0:512])
        XQL1r2 = XQL1.rearrange("(dk p) t -> p dk t", p=128)
        for dk in range(NDK):
            ctx.dma("sync", xql1r[:, dk], XQL1r2[:, dk, :], s_ld)
        ctx.barrier()

        with (
            nc.sbuf_tensor("p8_w", [128, 2, 2, 16, 128], BF16) as w1g,
            nc.sbuf_tensor("p8_u", [128, T], F32) as u1g,
            nc.sbuf_tensor("p8_t", [128, T], F32) as t1g,
            nc.sbuf_tensor("p8_mx", [128, T], F32) as m1g,
        ):
            wq8 = w1g.ap()
            u8 = u1g.ap()
            t8 = t1g.ap()
            mx8 = m1g.ap()
            WGQr = WGQ.rearrange("(dk p) o -> p dk o", p=128)
            WUQr = WUQ.rearrange("(dk p) o -> p dk o", p=128)
            H2Fr = H2F.rearrange("(oc p) t -> p oc t", p=128)
            wmk8 = {}

            def lp8(oc):
                r = oc % 2
                dsl = dw0 if r == 0 else dw1
                if oc >= 2:
                    ctx.wait("sync", s_pe, wmk8[oc - 2])
                ctx.dma("sync", wq8[:, r, 0],
                        WGQr[:, :, oc * 128:(oc + 1) * 128], dsl)
                return ctx.dma("sync", wq8[:, r, 1],
                               WUQr[:, :, oc * 128:(oc + 1) * 128], dsl)

            wld = {0: lp8(0), 1: lp8(1)}
            ev8 = (0, 0)
            st8 = {}
            for oc in range(64):
                r = oc % 2
                dsl = dw0 if r == 0 else dw1
                dss = dst0 if r == 0 else dst1
                ctx.wait("tensor", dsl, wld[oc])
                ctx.wait("tensor", s_v, ev8[0])
                ctx.wait("tensor", s_a, ev8[1])
                for (mi, qb) in ((0, 0), (1, 4)):
                    for dk in range(NDK):
                        for tb in range(NTB):
                            fin = (dk == NDK - 1 and tb == NTB - 1)
                            def f8(e, r=r, mi=mi, dk=dk, tb=tb, qb=qb, fi=fin):
                                m = e.matmul(ps8[:, qb + tb, 0:512],
                                             wq8[:, r, mi, dk],
                                             xql1r[:, dk,
                                                   tb * 512:(tb + 1) * 512],
                                             start=(dk == 0),
                                             stop=(dk == NDK - 1))
                                if fi:
                                    m.then_inc(s_pe.h, 1)
                            ctx.op("tensor", f8)
                            if fin:
                                s_pe.bump()
                mmd8 = s_pe.n
                wmk8[oc] = mmd8
                if oc + 2 < 64:
                    wld[oc + 2] = lp8(oc + 2)
                ygf = ps8.rearrange("p b n -> p (b n)")[:, 0:T]
                yuf = ps8.rearrange("p b n -> p (b n)")[:, T:2 * T]
                ctx.wait("vector", s_pe, mmd8)
                if oc >= 1:
                    pdss = dst0 if (oc - 1) % 2 == 0 else dst1
                    ctx.wait("vector", pdss, st8[oc - 1])
                vop("vector", "tensor_mul", u8, yuf, rbu)
                vop("vector", "tensor_mul", t8, ygf, rbg, inc=(s_v, 1))
                ctx.wait("scalar", s_v, s_v.n)
                act("scalar", t8, t8, AF.Silu, inc=(s_a, 1))
                ctx.wait("vector", s_a, s_a.n)
                vop("vector", "tensor_mul", u8, t8, u8)
                vop("vector", "tensor_tensor", t8, u8, u8, ALU.mult)
                if oc == 0:
                    vop("vector", "tensor_copy", mx8, t8, inc=(s_v, 1))
                else:
                    vop("vector", "tensor_max", mx8, mx8, t8, inc=(s_v, 1))
                ev8 = (s_v.n, s_a.n)
                ctx.wait("sync", s_v, s_v.n)
                st8[oc] = ctx.dma("sync", H2Fr[:, oc, :], u8, dss)
            ctx.barrier()

            for tt in range(NTT):
                bk = tt % 2
                if tt >= 2:
                    ctx.wait("tensor", s_v, s_v.n)
                ctx.op("tensor", lambda e, tt=tt, bk=bk: e.transpose(
                    ps8[:, bk, 0:128], mx8[:, tt * 128:(tt + 1) * 128],
                    idf32).then_inc(s_pe.h, 1))
                s_pe.bump()
                ctx.wait("vector", s_pe, s_pe.n)
                vtiny("vector", "tensor_reduce", msq3[:, tt:tt + 1],
                      ps8[:, bk, 0:128], AX.X, ALU.max, False, inc=(s_v, 1))
            ctx.wait("scalar", s_v, s_v.n)
            act("scalar", msq3, msq3, AF.Sqrt, bias=0.0, scale=1.0,
                inc=(s_a, 1), drain=True)
            ctx.wait("vector", s_a, s_a.n)
            vtiny("vector", "tensor_scalar", msq3, msq3, 1.0 / 127.0, 1e-5,
                  ALU.mult, ALU.add)
            vtiny("vector", "reciprocal", rq1, msq3)
            vtiny("vector", "tensor_single_scalar", m3c, msq3, thrcol[:, 5:6],
                  ALU.mult, inc=(s_v, 1))
            ctx.barrier()

        with nc.sbuf_tensor("p8_rowB", [1, 512], F32) as rg:
            build_rowbcast(rq1, NTT, rg.ap(), rbg, ps8[:, 0, 0:512])
        ctx.barrier()
        with (
            nc.sbuf_tensor("p8_hgi", [128, 2, T], F32) as r1g,
            nc.sbuf_tensor("p8_t3", [128, T], F32) as r2g,
            nc.sbuf_tensor("p8_u3", [128, T], F32) as r3g,
            nc.sbuf_tensor("p8_c3", [128, T], F32) as r4g,
            nc.sbuf_tensor("p8_f3", [128, T], F32) as r5g,
            nc.sbuf_tensor("p8_q3", [128, 2, T], BF16) as r6g,
        ):
            hgi = r1g.ap()
            t3 = r2g.ap()
            u3 = r3g.ap()
            c3 = r4g.ap()
            f3 = r5g.ap()
            q3 = r6g.ap()
            H2Fr = H2F.rearrange("(oc p) t -> p oc t", p=128)
            XQH2r = XQH2.rearrange("(oc p) t -> p oc t", p=128)
            vmk8 = {}
            smk8 = {}
            for oc in range(64):
                pb = oc % 2
                dsl = dx0 if pb == 0 else dx1
                dss = dst0 if pb == 0 else dst1
                if oc >= 2:
                    ctx.wait("sync", s_v, vmk8[oc - 2])
                tgt = ctx.dma("sync", hgi[:, pb], H2Fr[:, oc, :], dsl)
                ctx.wait("vector", dsl, tgt)
                if oc >= 2:
                    ctx.wait("vector", dss, smk8[oc - 2])
                vop("vector", "tensor_mul", t3, hgi[:, pb], rbg)
                rne_ladder("vector", t3, q3[:, pb], u3, c3, f3, inc=(s_v, 1))
                vmk8[oc] = s_v.n
                ctx.wait("sync", s_v, s_v.n)
                smk8[oc] = ctx.dma("sync", XQH2r[:, oc, :], q3[:, pb], dss)
            ctx.barrier()

    with (
        nc.sbuf_tensor("p8d_xq", [128, 64, 1024], BF16) as s1g,
        nc.sbuf_tensor("p8d_w", [128, 2, 128], BF16) as s2g,
        nc.sbuf_tensor("p8d_mb", [128, 2, 1024], F32) as s3g,
        nc.sbuf_tensor("p8d_x1", [128, 2, 8, 128], F32) as s4g,
        nc.sbuf_tensor("p8d_s1", [128, 2, 8, 128], F32) as s5g,
        nc.sbuf_tensor("p8d_g", [128, 2, 128], F32) as s6g,
        nc.sbuf_tensor("p8d_o", [128, 2, 128], F32) as s7g,
        nc.psum_tensor("p8d_ps", [128, 2, 512], F32) as s8g,
        nc.psum_tensor("p8d_pt", [128, 4, 512], F32) as s9g,
    ):
        xq8 = s1g.ap()
        wd8 = s2g.ap()
        mb8 = s3g.ap()
        x1b = s4g.ap()
        s1b = s5g.ap()
        g8 = s6g.ap()
        o8 = s7g.ap()
        ps8d = s8g.ap()
        ps8t = s9g.ap()
        XQH2r2 = XQH2.rearrange("(ic p) t -> p ic t", p=128)
        X1r = X1.rearrange("(tc p) d -> p tc d", p=128)
        S1Br2 = S1B.rearrange("(tc p) d -> p tc d", p=128)
        outr = out.rearrange("(tc p) d -> p tc d", p=128)
        wld8 = {}
        mmk8 = {}
        evm8 = (0, 0)
        tpm8 = {}
        stm8 = {}
        ldm8 = {}

        def lpd(j):
            dc, ic = (j % 1024) // 64, j % 64
            r = j % 2
            dsl = dw0 if r == 0 else dw1
            if j >= 2:
                ctx.wait("sync", s_pe, mmk8[j - 2])
            wld8[j] = ctx.dma("sync", wd8[:, r],
                              WDQ[ic * 128:(ic + 1) * 128,
                                  dc * 128:(dc + 1) * 128], dsl)

        for th in range(2):
            if th == 1:
                ctx.wait("sync", s_pe, s_pe.n)
            txq = ctx.dma("sync", xq8, XQH2r2[:, :, th * 1024:(th + 1) * 1024],
                          dcs)
            ctx.wait("tensor", dcs, txq)
            if th == 0:
                lpd(0)
                lpd(1)
            for dc in range(16):
                dcg = th * 16 + dc
                xsl = dc % 2
                dxl = dx0 if xsl == 0 else dx1
                if dcg >= 2:
                    ctx.wait("sync", s_v, ldm8[dcg - 2])
                ctx.dma("sync", x1b[:, xsl],
                        X1r[:, th * 8:(th + 1) * 8, dc * 128:(dc + 1) * 128],
                        dxl)
                tx1 = ctx.dma("sync", s1b[:, xsl],
                              S1Br2[:, th * 8:(th + 1) * 8,
                                    dc * 128:(dc + 1) * 128], dxl)
                for ic in range(64):
                    j = th * 1024 + dc * 64 + ic
                    r = j % 2
                    dsl = dw0 if r == 0 else dw1
                    ctx.wait("tensor", dsl, wld8[j])
                    if ic == 0:
                        ctx.wait("tensor", s_v, evm8[0])
                        ctx.wait("tensor", s_a, evm8[1])
                    for tb in range(2):
                        fin = (ic == 63 and tb == 1)
                        def fd(e, r=r, ic=ic, tb=tb, fi=fin):
                            m = e.matmul(ps8d[:, tb, 0:512], wd8[:, r],
                                         xq8[:, ic, tb * 512:(tb + 1) * 512],
                                         start=(ic == 0), stop=(ic == 63))
                            if fi:
                                m.then_inc(s_pe.h, 1)
                        ctx.op("tensor", fd)
                        if fin:
                            s_pe.bump()
                    mmk8[j] = s_pe.n
                    if j + 2 < 2048:
                        lpd(j + 2)
                mmd = s_pe.n
                msl = dc % 2
                ctx.wait("vector", s_pe, mmd)
                if dcg >= 2:
                    ctx.wait("vector", s_pe, tpm8[dcg - 2])
                vop("vector", "tensor_copy", mb8[:, msl],
                    ps8d.rearrange("p b n -> p (b n)")[:, 0:1024],
                    inc=(s_v, 1))
                cpd = s_v.n
                evm8 = (s_v.n, 0)
                ctx.wait("tensor", s_v, cpd)
                ctx.wait("vector", dxl, tx1)
                for tc8 in range(8):
                    tt = th * 8 + tc8
                    bk = tc8 % 4
                    gi = dcg * 8 + tc8
                    if gi >= 4:
                        ctx.wait("tensor", s_v, tpm8.get(("c", gi - 4), 0))
                    def ftd(e, bk=bk, msl=msl, tc8=tc8):
                        e.transpose(ps8t[:, bk, 0:128],
                                    mb8[:, msl, tc8 * 128:(tc8 + 1) * 128],
                                    idf32).then_inc(s_pe.h, 1)
                    ctx.op("tensor", ftd)
                    s_pe.bump()
                    ctx.wait("scalar", s_pe, s_pe.n)
                    if gi >= 2:
                        ctx.wait("scalar", s_v, tpm8.get(("c", gi - 2), 0))
                    act("scalar", g8[:, tc8 % 2], ps8t[:, bk, 0:128], AF.Silu,
                        scale=m3c[:, tt:tt + 1], inc=(s_a, 1))
                    ctx.wait("vector", s_a, s_a.n)
                    if gi >= 2:
                        ctx.wait("vector", dst0 if gi % 2 == 0 else dst1,
                                 stm8.get(gi - 2, 0))
                    vop("vector", "tensor_tensor", o8[:, tc8 % 2],
                        g8[:, tc8 % 2], s1b[:, xsl, tc8], ALU.mult)
                    vop("vector", "tensor_tensor", o8[:, tc8 % 2],
                        o8[:, tc8 % 2], x1b[:, xsl, tc8], ALU.add,
                        inc=(s_v, 1))
                    tpm8[("c", gi)] = s_v.n
                    ctx.wait("sync", s_v, s_v.n)
                    stm8[gi] = ctx.dma(
                        "sync", outr[:, tt, dc * 128:(dc + 1) * 128],
                        o8[:, tc8 % 2], dst0 if gi % 2 == 0 else dst1)
                ldm8[dcg] = s_v.n
                tpm8[dcg] = s_pe.n
        ctx.barrier()

    with nc.Block() as block:
        @block.sync
        def _(eng):
            for f in ctx.q["sync"]:
                f(eng)

        @block.gpsimd
        def _(eng):
            for f in ctx.q["gpsimd"]:
                f(eng)

        @block.vector
        def _(eng):
            for fi, f in enumerate(ctx.q["vector"]):
                try:
                    f(eng)
                except Exception:
                    import sys
                    print(f"VECTOR op #{fi} failed:",
                          getattr(f, "__defaults__", None), file=sys.stderr)
                    raise

        @block.scalar
        def _(eng):
            for f in ctx.q["scalar"]:
                f(eng)

        @block.tensor
        def _(eng):
            for f in ctx.q["tensor"]:
                f(eng)

    for cm in reversed(sem_ctxs):
        cm.__exit__(None, None, None)
    return nc



def prep_consts():
    c = {}
    c["idf32"] = np.eye(128, dtype=np.float32)
    c["idbf"] = np.eye(128, dtype=BF16NP)
    ut = np.triu(np.ones((128, 128), dtype=np.float32))
    c["utones"] = ut.astype(BF16NP)
    c["allones"] = np.ones((128, 128), dtype=BF16NP)
    lm = np.where(np.arange(128)[:, None] <= np.arange(128)[None, :],
                  0.0, NEG).astype(np.float32)
    c["lmask"] = lm
    cm = np.zeros((128, 4, 512), dtype=np.float32)
    for v in range(4):
        q = v * 128 + np.arange(128)[:, None]
        k = np.arange(512)[None, :]
        cm[:, v, :] = np.where(q >= k, 0.0, NEG)
    c["cmask"] = cm
    return c


def prep_inputs(ins):
    f32 = np.float32
    m = {}
    m["inW"] = np.ascontiguousarray(ins["L0_attn_in_proj_w"].T)
    m["outW"] = np.ascontiguousarray(ins["L0_attn_out_proj_w"].T)
    m["gateW"] = np.ascontiguousarray(ins["L0_attn_gate_w"].T)
    m["upW"] = np.ascontiguousarray(ins["L0_attn_up_w"].T)
    m["downW"] = np.ascontiguousarray(ins["L0_attn_down_w"].T)
    m["w1W"] = np.ascontiguousarray(ins["L0_ffn_w1"].T)
    m["w2W"] = np.ascontiguousarray(ins["L0_ffn_w2"].T)
    m["w3W"] = np.ascontiguousarray(ins["L0_ffn_w3"].T)
    m["wqW"] = np.ascontiguousarray(ins["L1_attn_Wq"])
    m["wkW"] = np.ascontiguousarray(ins["L1_attn_Wk"])
    m["wvW"] = np.ascontiguousarray(ins["L1_attn_Wv"])
    m["woW"] = np.ascontiguousarray(ins["L1_attn_Wo"])
    m["wgW"] = np.ascontiguousarray(ins["L1_ffn_w_gate"].T)
    m["wuW"] = np.ascontiguousarray(ins["L1_ffn_w_up"].T)
    m["wdW"] = np.ascontiguousarray(ins["L1_ffn_w_down"].T)
    m["convw"] = np.ascontiguousarray(
        ins["L0_attn_conv_w"].reshape(34, 128, 4).transpose(1, 0, 2)).astype(f32)
    m["convb"] = np.ascontiguousarray(
        ins["L0_attn_conv_b"].reshape(34, 128).T).astype(f32)
    m["dtbias_c"] = np.ascontiguousarray(ins["L0_attn_dt_bias"].reshape(64, 1)).astype(f32)
    m["alog_b"] = np.ascontiguousarray(
        np.broadcast_to(ins["L0_attn_A_log"][None, :], (128, 64))).astype(f32)
    m["dpar_b"] = np.ascontiguousarray(
        np.broadcast_to(ins["L0_attn_D_param"][None, :], (128, 64))).astype(f32)
    m["lam_b"] = np.ascontiguousarray(
        np.broadcast_to(ins["L1_attn_lambda_init"][None, :], (128, 16))).astype(f32)
    m["ln1w_c"] = np.ascontiguousarray(ins["L0_attn_ln1_w"].reshape(16, 128).T).astype(f32)
    m["ln2w_row"] = np.ascontiguousarray(
        np.broadcast_to(ins["L0_attn_ln2_w"][None, :], (128, D))).astype(f32)
    m["normw_col"] = np.ascontiguousarray(
        ins["L0_attn_norm_w"].reshape(32, 128).T).astype(f32)
    m["g1c"] = np.ascontiguousarray(ins["L0_ffn_g1"].reshape(16, 128).T).astype(f32)
    m["g1row"] = np.ascontiguousarray(
        np.broadcast_to(ins["L0_ffn_g1"][None, :], (128, D))).astype(f32)
    m["g2row"] = np.ascontiguousarray(
        np.broadcast_to(ins["L0_ffn_g2"][None, :], (128, D))).astype(f32)
    m["g2c"] = np.ascontiguousarray(ins["L0_ffn_g2"].reshape(16, 128).T).astype(f32)
    m["g3c"] = np.ascontiguousarray(ins["L0_ffn_g3"].reshape(64, 128).T).astype(f32)
    m.update(prep_consts())
    return m


def kernel(**inputs):
    shared = prep_inputs(inputs)
    x = np.ascontiguousarray(inputs["L0_attn_hidden_states"], dtype=np.float32)
    nc = build_program()
    in_maps = []
    for i in range(8):
        mi = dict(shared)
        mi["x0"] = x[i]
        in_maps.append(mi)
    res = run_bass_kernel_spmd(nc, in_maps, core_ids=list(range(8)))
    return np.stack([r["out"] for r in res.results], axis=0)


if __name__ == "__main__":
    pass
